# version 02
# baseline (speedup 1.0000x reference)
import math
import numpy as np
import concourse.bass as bass
import concourse.mybir as mybir
from concourse.bass_utils import run_bass_kernel_spmd

F32 = mybir.dt.float32
F32R = mybir.dt.float32r
BF16 = mybir.dt.bfloat16
I32 = mybir.dt.int32
AF = mybir.ActivationFunctionType
ALU = mybir.AluOpType
AX = mybir.AxisListType

B, T, D = 4, 4096, 1024
TL = 2048
W4_ = 4096
NC = 8
PI = float(np.pi)


ENGINES = ("sync", "tensor", "vector", "scalar", "gpsimd")
STRICT_SAME_ENGINE = False
A0_SKIP_INTER = False
A0_NCHUNK = 4
A0_NGRP = 4


class Buf:
    __slots__ = ("w", "readers", "name")

    def __init__(self, name=""):
        self.w = None
        self.readers = []
        self.name = name

    def read_deps(self):
        return [self.w] if self.w is not None else []

    def write_deps(self):
        deps = [self.w] if self.w is not None else []
        return deps + self.readers


class Prog:
    def __init__(self, nc):
        self.nc = nc
        self.ops = {e: [] for e in ENGINES}
        self.engine_sem = {"tensor": "tC", "vector": "vC", "scalar": "sC",
                          "gpsimd": "gC"}
        self.dma_sem = {"sync": "syD", "gpsimd": "gD", "scalar": "scD"}
        self.sem_names = ["tC", "vC", "sC", "gC", "syD", "gD", "scD"]
        self.plan = {s: 0 for s in self.sem_names}
        self.hwm = {e: {} for e in ENGINES}
        self.n_ops = 0
        self.deps_log = []

    def op(self, engine, fn, reads=(), writes=(), after=(), dma=False):
        assert engine in ENGINES
        if dma:
            sem = self.dma_sem[engine]
            inc = 16
        else:
            sem = self.engine_sem[engine]
            inc = 1
        deps = set(after)
        for b in reads:
            deps.update(b.read_deps())
        for b in writes:
            deps.update(b.write_deps())
        waits = {}
        own_sem = self.engine_sem.get(engine)
        for (s, v, peng, pdma) in deps:
            assert v <= self.plan[s], "dependency on future op"
            if peng == engine and not pdma and not dma:
                if not STRICT_SAME_ENGINE:
                    continue
                waits[s] = max(waits.get(s, 0), v)
                continue
            if peng == engine and not pdma and dma:
                continue
            waits[s] = max(waits.get(s, 0), v)
        wl = [(s, v) for (s, v) in waits.items()]
        self.plan[sem] += inc
        token = (sem, self.plan[sem], engine, dma)
        for b in reads:
            b.readers.append(token)
        for b in writes:
            b.w = token
            b.readers = []
        self.ops[engine].append((wl, fn, sem, inc, self.n_ops))
        self.deps_log.append((self.n_ops, engine, sem, inc, tuple(deps), tuple(wl)))
        self.n_ops += 1
        return token

    def emit(self, sems):
        for eng_name in ENGINES:
            oplist = self.ops[eng_name]
            hwm = {}
            for (wl, fn, sem, inc, _idx) in oplist:
                pass
        return

    def verify(self):
        n = self.n_ops
        info = {}
        for (idx, eng, sem, inc, deps, wl) in self.deps_log:
            info[idx] = (eng, sem, inc, deps, wl)
        cum = {}
        achieve = {}
        order = sorted(info)
        for idx in order:
            eng, sem, inc, deps, wl = info[idx]
            cum[sem] = cum.get(sem, 0) + inc
            achieve[(sem, cum[sem])] = idx
        import collections
        preds = collections.defaultdict(set)
        last_on_engine = {}
        for idx in order:
            eng, sem, inc, deps, wl = info[idx]
            if eng in last_on_engine:
                preds[idx].add(last_on_engine[eng])
            last_on_engine[eng] = idx
            for (s, v) in wl:
                vv = v
                while (s, vv) not in achieve and vv <= self.plan[s]:
                    vv += 1
                if (s, vv) in achieve:
                    preds[idx].add(achieve[(s, vv)])
        reach = [0] * n
        for idx in order:
            r = 0
            for p in preds[idx]:
                r |= reach[p] | (1 << p)
            reach[idx] = r
        bad = 0
        for idx in order:
            eng, sem, inc, deps, wl = info[idx]
            for (s, v, peng, pdma) in deps:
                pidx = achieve.get((s, v))
                if pidx is None:
                    continue
                if not (reach[idx] >> pidx) & 1:
                    print(f"MISSING EDGE: op{idx}({eng}) needs op{pidx}({peng}) "
                          f"[{s}>={v}] waits={wl}")
                    bad += 1
                    if bad > 20:
                        return False
        return bad == 0

    def make_emitter(self, eng_name, sems):
        oplist = self.ops[eng_name]

        def emitter(eng):
            hwm = {}
            for (wl, fn, semname, inc, _idx) in oplist:
                for (s, v) in wl:
                    if hwm.get(s, 0) >= v:
                        continue
                    eng.wait_ge(sems[s], v)
                    hwm[s] = v
                last = fn(eng)
                assert last is not None, f"op on {eng_name} returned None"
                last.then_inc(sems[semname], inc)
        return emitter


def build_and_finalize(nc, prog):
    import contextlib
    with contextlib.ExitStack() as st:
        sems = {}
        for s in prog.sem_names:
            sems[s] = st.enter_context(nc.semaphore(s))
        blk = st.enter_context(nc.Block())
        blk.sync(prog.make_emitter("sync", sems))
        blk.tensor(prog.make_emitter("tensor", sems))
        blk.vector(prog.make_emitter("vector", sems))
        blk.scalar(prog.make_emitter("scalar", sems))
        blk.gpsimd(prog.make_emitter("gpsimd", sems))



class Ctx:

    b0_variant_q = staticmethod(lambda hp, qt, seg, prefix: b0_variant_q(hp, qt, seg, prefix))
    b0_variant_t = staticmethod(lambda hp, kc, qtb: b0_variant_t(hp, kc, qtb))

    def __init__(self, nc, prog):
        self.nc = nc
        self.P = prog
        self.sb = {}
        self.dram = {}
        self.ps = []
        self.ln_banks = (6, 7)
        self.big_ps = None
        self.y_out = None
        self.guards = []

    def alloc_sb(self, stack, name, shape, dtype):
        h = stack.enter_context(self.nc.sbuf_tensor(name, shape, dtype))
        self.sb[name] = (h, Buf(name))
        return h

    def get(self, name):
        return self.sb[name]

    def alloc_psum(self, stack, n):
        for i in range(n):
            h = stack.enter_context(self.nc.psum_tensor(f"psb{i}", [128, 512], F32))
            self.ps.append((h, Buf(f"psb{i}")))


def emit_ln_tg(ctx, src, src_b, src_nch, src_t0, dst, dst_b, dst_nch, dst_t0,
               *, eps_col=0, dst_bc16=False, src_bcf32=False):
    nc, P = ctx.nc, ctx.P
    x2, x2_b = ctx.get("lnx2")
    xb, xb_b = ctx.get("lnxb")
    row, row_b = ctx.get("lnrow")
    cst, cst_b = ctx.get("consts")
    ones_col, onc_b = ctx.get("ones_colb")
    ones_row, _ = ctx.get("ones_row")
    tmp, tmp_b = ctx.get("lntmp")
    psA, psA_b = ctx.ps[ctx.ln_banks[0]]
    psB, psB_b = ctx.ps[ctx.ln_banks[1]]

    srcW = src.ap().shape[1] // src_nch
    dstW = dst.ap().shape[1] // dst_nch

    def sap():
        if src_bcf32:
            return src.ap().bitcast(F32)[:, 0:src_nch * 512]
        return src.ap()

    def s_ap(c):
        return sap().rearrange("p (c t) -> p c t", c=src_nch)[:, c, src_t0:src_t0 + 512]

    def d_ap(c):
        dap = dst.ap().bitcast(BF16) if dst_bc16 else dst.ap()
        return dap.rearrange("p (c t) -> p c t", c=dst_nch)[:, c, dst_t0:dst_t0 + 512]

    def f_sq(eng):
        i_ap = sap().rearrange("p (c t) -> p c t", c=src_nch)[:, :, src_t0:src_t0 + 512]
        o_ap = x2.ap().rearrange("p (c t) -> p c t", c=NC)
        return eng.activation(out=o_ap, in_=i_ap, func=AF.Square)
    P.op("scalar", f_sq, reads=[src_b], writes=[x2_b])

    def f_cast(eng):
        i_ap = sap().rearrange("p (c t) -> p c t", c=src_nch)[:, :, src_t0:src_t0 + 512]
        o_ap = xb.ap().rearrange("p (c t) -> p c t", c=NC)
        return eng.tensor_copy(o_ap, i_ap)
    P.op("vector", f_cast, reads=[src_b], writes=[xb_b])

    def f_sums(eng):
        last = None
        for c in range(NC):
            rhs = xb.ap().rearrange("p (c t) -> p c t", c=NC)[:, c, :]
            last = eng.matmul(psA.ap()[0:1, :], ones_col.ap(), rhs,
                              start=(c == 0), stop=(c == NC - 1))
        for c in range(NC):
            rhs = x2.ap().rearrange("p (c t) -> p c t", c=NC)[:, c, :]
            last = eng.matmul(psB.ap()[0:1, :], ones_col.ap(), rhs,
                              start=(c == 0), stop=(c == NC - 1))
        return last
    P.op("tensor", f_sums, reads=[xb_b, x2_b], writes=[psA_b, psB_b])

    def seg(s_):
        return row.ap()[0:1, s_ * 512:s_ * 512 + 512]

    def f_stats(eng):
        eng.tensor_scalar_mul(seg(0), psA.ap()[0:1, :], 1.0 / D)
        eng.tensor_scalar_mul(seg(1), psB.ap()[0:1, :], 1.0 / D)
        eng.tensor_tensor(tmp.ap()[0:1, 0:512], seg(0), seg(0), ALU.mult)
        return eng.tensor_sub(seg(1), seg(1), tmp.ap()[0:1, 0:512])
    P.op("vector", f_stats, reads=[psA_b, psB_b], writes=[row_b, tmp_b])

    def f_rstd(eng):
        eng.activation(out=seg(1), in_=seg(1), func=AF.Ln,
                       bias=cst.ap()[0:1, eps_col:eps_col + 1])
        return eng.activation(out=seg(1), in_=seg(1), func=AF.Exp, scale=-0.5)
    P.op("scalar", f_rstd, reads=[row_b, cst_b], writes=[row_b])

    def f_b(eng):
        return eng.scalar_tensor_tensor(seg(0), seg(0), -1.0, seg(1), ALU.mult, ALU.mult)
    P.op("vector", f_b, reads=[row_b], writes=[row_b])

    def f_bc_a(eng):
        return eng.matmul(psB.ap()[:, :], ones_row.ap(), seg(1), start=True, stop=True)
    P.op("tensor", f_bc_a, reads=[row_b], writes=[psB_b])

    def f_bc_b(eng):
        return eng.matmul(psA.ap()[:, :], ones_row.ap(), seg(0), start=True, stop=True)
    P.op("tensor", f_bc_b, reads=[row_b], writes=[psA_b])

    for c in range(NC):
        def f_m(eng, c=c):
            return eng.tensor_tensor(tmp.ap(), s_ap(c), psB.ap(), ALU.mult)
        P.op("vector", f_m, reads=[src_b, psB_b], writes=[tmp_b])

        def f_a(eng, c=c):
            return eng.tensor_tensor(d_ap(c), tmp.ap(), psA.ap(), ALU.add)
        P.op("vector", f_a, reads=[tmp_b, psA_b], writes=[dst_b])


def emit_ln(ctx, src_name, dst_name, *, n_tok=TL, eps_col=0):
    src, src_b = ctx.get(src_name)
    dst, dst_b = ctx.get(dst_name)
    for tg in range(n_tok // 512):
        emit_ln_tg(ctx, src, src_b, NC, tg * 512, dst, dst_b, NC, tg * 512,
                   eps_col=eps_col)


def emit_proj(ctx, src_name, dst_name, w_name, *, n_tok=TL, bias_name=None,
              bias_col=0, act=None, act_scale=1.0, add_to=None, dst_f32=False,
              w_col_off=0, n_dout=NC, psum_banks=(0, 1), evac="vector",
              src_bufname=None):
    nc, P = ctx.nc, ctx.P
    src, src_b = ctx.get(src_name)
    if src_bufname is not None:
        _, src_b = ctx.get(src_bufname)
    dst, dst_b = ctx.get(dst_name)
    w, w_b = ctx.get(w_name)
    wcols = w.ap().shape[1] // NC
    if bias_name is not None:
        bias, bias_b = ctx.get(bias_name)
    n_tg = n_tok // 512
    nbank = len(psum_banks)
    for do in range(n_dout):
        for tg in range(n_tg):
            bank = psum_banks[(do * n_tg + tg) % nbank]
            psh, psb = ctx.ps[bank]

            def f_mm(eng, do=do, tg=tg, psh=psh):
                last = None
                for dc in range(NC):
                    lhsT = w.ap()[:, w_col_off + dc * wcols + do * 128:
                                  w_col_off + dc * wcols + do * 128 + 128]
                    rhs = src.ap().rearrange("p (c t) -> p c t", c=NC)[:, dc, tg * 512:(tg + 1) * 512]
                    last = eng.matmul(psh.ap(), lhsT, rhs,
                                      start=(dc == 0), stop=(dc == NC - 1))
                return last
            P.op("tensor", f_mm, reads=[src_b, w_b], writes=[psb])

            o_ap = dst.ap().rearrange("p (c t) -> p c t", c=n_dout)[:, do, tg * 512:(tg + 1) * 512]

            if act is not None:
                def f_ev(eng, do=do, tg=tg, psh=psh, o_ap=o_ap):
                    kw = {}
                    if bias_name is not None:
                        kw["bias"] = bias.ap()[:, bias_col + do:bias_col + do + 1]
                    return eng.activation(out=o_ap, in_=psh.ap(), func=act,
                                          scale=act_scale, **kw)
                reads = [psb] + ([bias_b] if bias_name is not None else [])
                P.op("scalar", f_ev, reads=reads, writes=[dst_b])
            elif add_to:
                def f_ev(eng, do=do, tg=tg, psh=psh, o_ap=o_ap):
                    return eng.tensor_add(o_ap, o_ap, psh.ap())
                P.op("vector", f_ev, reads=[psb, dst_b], writes=[dst_b])
            elif bias_name is not None:
                def f_ev(eng, do=do, tg=tg, psh=psh, o_ap=o_ap):
                    return eng.tensor_scalar_add(o_ap, psh.ap(),
                                                 bias.ap()[:, bias_col + do:bias_col + do + 1])
                P.op("vector" if evac == "vector" else "gpsimd", f_ev,
                     reads=[psb, bias_b], writes=[dst_b])
            else:
                def f_ev(eng, do=do, tg=tg, psh=psh, o_ap=o_ap):
                    return eng.tensor_copy(o_ap, psh.ap())
                P.op("vector" if evac == "vector" else "scalar", f_ev,
                     reads=[psb], writes=[dst_b])



def emit_ffn0(ctx):
    emit_ln(ctx, "xt", "xln")
    emit_proj(ctx, "xln", "hb", "w_ffn0_w")
    emit_proj(ctx, "hb", "xln", "w_ffn0_a1")
    emit_proj(ctx, "xln", "hb", "w_ffn0_a2", bias_name="b_ffn0", act=AF.Tanh)
    emit_addres(ctx, "hb")


def emit_addres(ctx, src_name, n_tok=TL):
    nc, P = ctx.nc, ctx.P
    src, src_b = ctx.get(src_name)
    xt, xt_b = ctx.get("xt")
    n_tg = n_tok // 512
    for c in range(NC):
        for tg in range(n_tg):
            def f(eng, c=c, tg=tg):
                o = xt.ap().rearrange("p (c t) -> p c t", c=NC)[:, c, tg * 512:(tg + 1) * 512]
                i = src.ap().rearrange("p (c t) -> p c t", c=NC)[:, c, tg * 512:(tg + 1) * 512]
                return eng.tensor_add(o, o, i)
            P.op("vector", f, reads=[src_b, xt_b], writes=[xt_b])



def _wmat(w):
    import ml_dtypes
    din, dout = w.shape
    a = np.ascontiguousarray(w.reshape(NC, 128, dout)).astype(ml_dtypes.bfloat16)
    return a


def _wvec(v):
    return np.ascontiguousarray(v.reshape(NC, 128).T).astype(np.float32)




def emit_ln_edge(ctx, eps_col=0):
    nc, P = ctx.nc, ctx.P
    edge, edge_b = ctx.get("edge_sb")
    eln, eln_b = ctx.get("edge_ln")
    xb, xb_b = ctx.get("lnxb")
    x2, x2_b = ctx.get("lnx2")
    row, row_b = ctx.get("lnrow")
    cst, cst_b = ctx.get("consts")
    ones_col, onc_b = ctx.get("ones_colb")
    ones_row, _ = ctx.get("ones_row")
    psA, psA_b = ctx.ps[ctx.ln_banks[0]]
    psB, psB_b = ctx.ps[ctx.ln_banks[1]]

    def rc(c):
        return row.ap()[0:1, c:c + 1]

    P.op("vector", lambda e: e.tensor_copy(xb.ap()[:, 0:NC], edge.ap()),
         reads=[edge_b], writes=[xb_b])
    P.op("scalar", lambda e: e.activation(out=x2.ap()[:, 0:NC], in_=edge.ap(), func=AF.Square),
         reads=[edge_b], writes=[x2_b])

    def f_sums(eng):
        eng.matmul(psA.ap()[0:1, 0:NC], ones_col.ap(), xb.ap()[:, 0:NC], start=True, stop=True)
        return eng.matmul(psB.ap()[0:1, 0:NC], ones_col.ap(), x2.ap()[:, 0:NC], start=True, stop=True)
    P.op("tensor", f_sums, reads=[xb_b, x2_b], writes=[psA_b, psB_b])

    def f_red(eng):
        eng.reduce_sum(rc(0), psA.ap()[0:1, 0:NC], axis=AX.X)
        return eng.reduce_sum(rc(4), psB.ap()[0:1, 0:NC], axis=AX.X)
    P.op("vector", f_red, reads=[psA_b, psB_b], writes=[row_b])

    def f_scale(eng):
        eng.activation(out=rc(8), in_=rc(0), func=AF.Copy, scale=1.0 / D)
        return eng.activation(out=rc(12), in_=rc(4), func=AF.Copy, scale=1.0 / D)
    P.op("scalar", f_scale, reads=[row_b], writes=[row_b])

    P.op("vector", lambda e: e.tensor_tensor(rc(16), rc(8), rc(8), ALU.mult),
         reads=[row_b], writes=[row_b])
    P.op("gpsimd", lambda e: e.tensor_sub(rc(20), rc(12), rc(16)),
         reads=[row_b], writes=[row_b])
    P.op("scalar", lambda e: e.activation(out=rc(24), in_=rc(20), func=AF.Ln,
                                          bias=cst.ap()[0:1, eps_col:eps_col + 1]),
         reads=[row_b, cst_b], writes=[row_b])
    P.op("vector", lambda e: e.tensor_copy(rc(32), rc(24)),
         reads=[row_b], writes=[row_b])
    P.op("scalar", lambda e: e.activation(out=rc(28), in_=rc(32), func=AF.Exp, scale=-0.5),
         reads=[row_b], writes=[row_b])

    def f_bc(eng):
        eng.matmul(psB.ap()[:, 0:1], ones_row.ap(), rc(8), start=True, stop=True)
        return eng.matmul(psB.ap()[:, 1:2], ones_row.ap(), rc(28), start=True, stop=True)
    P.op("tensor", f_bc, reads=[row_b], writes=[psB_b])

    P.op("vector", lambda e: e.tensor_scalar(eln.ap(), edge.ap(), psB.ap()[:, 0:1],
                                             psB.ap()[:, 1:2], ALU.subtract, ALU.mult),
         reads=[edge_b, psB_b], writes=[eln_b])


def emit_ffn1(ctx):
    nc, P = ctx.nc, ctx.P
    emit_ln(ctx, "xt", "xln")
    emit_ln_edge(ctx)
    xln, xln_b = ctx.get("xln")
    eln, eln_b = ctx.get("edge_ln")
    xp, xp_b = ctx.get("sg0")
    xr, xr_b = ctx.get("sg1")
    xk, xk_b = ctx.get("sg2")
    R, R_b = ctx.get("sg3")
    KB, KB_b = ctx.get("lnxb")
    K2, K2_b = ctx.get("lnx2")
    mur, mur_b = ctx.get("v_mu_r")
    muk, muk_b = ctx.get("v_mu_k")
    tmp, tmp_b = ctx.get("lntmp")
    xt, xt_b = ctx.get("xt")
    w_r, w_r_b = ctx.get("w_ffn1_r")
    w_k, w_k_b = ctx.get("w_ffn1_k")
    w_v, w_v_b = ctx.get("w_ffn1_v")

    def xlnc(c, t0, w=512):
        return xln.ap().rearrange("p (c t) -> p c t", c=NC)[:, c, t0:t0 + w]

    for tg in range(4):
        t0 = tg * 512
        for c in range(NC):
            def f_shift(eng, c=c, t0=t0):
                o = xp.ap().rearrange("p (c t) -> p c t", c=NC)
                if t0 == 0:
                    eng.tensor_copy(o[:, c, 1:512], xlnc(c, 0, 511))
                    return eng.tensor_copy(o[:, c, 0:1], eln.ap()[:, c:c + 1])
                return eng.tensor_copy(o[:, c, 0:512], xlnc(c, t0 - 1))
            P.op("vector", f_shift, reads=[xln_b, eln_b], writes=[xp_b])
        for c in range(NC):
            def f_d(eng, c=c, t0=t0):
                return eng.tensor_sub(tmp.ap(),
                                      xlnc(c, t0),
                                      xp.ap().rearrange("p (c t) -> p c t", c=NC)[:, c, :])
            P.op("vector", f_d, reads=[xln_b, xp_b], writes=[tmp_b])

            def f_xr(eng, c=c, t0=t0):
                return eng.scalar_tensor_tensor(
                    xr.ap().rearrange("p (c t) -> p c t", c=NC)[:, c, :],
                    tmp.ap(), mur.ap()[:, c:c + 1],
                    xp.ap().rearrange("p (c t) -> p c t", c=NC)[:, c, :],
                    ALU.mult, ALU.add)
            P.op("vector", f_xr, reads=[tmp_b, xp_b, mur_b], writes=[xr_b])

            def f_xk(eng, c=c, t0=t0):
                return eng.scalar_tensor_tensor(
                    xk.ap().rearrange("p (c t) -> p c t", c=NC)[:, c, :],
                    tmp.ap(), muk.ap()[:, c:c + 1],
                    xp.ap().rearrange("p (c t) -> p c t", c=NC)[:, c, :],
                    ALU.mult, ALU.add)
            P.op("vector", f_xk, reads=[tmp_b, xp_b, muk_b], writes=[xk_b])

        for do in range(NC):
            psh, psb = ctx.ps[do % 2]

            def f_mm_r(eng, do=do, psh=psh):
                last = None
                for dc in range(NC):
                    lhsT = w_r.ap()[:, dc * D + do * 128: dc * D + do * 128 + 128]
                    rhs = xr.ap().rearrange("p (c t) -> p c t", c=NC)[:, dc, :]
                    last = eng.matmul(psh.ap(), lhsT, rhs, start=(dc == 0), stop=(dc == NC - 1))
                return last
            P.op("tensor", f_mm_r, reads=[xr_b, w_r_b], writes=[psb])

            def f_sig(eng, do=do, psh=psh):
                o = R.ap().rearrange("p (c t) -> p c t", c=NC)[:, do, :]
                return eng.activation(out=o, in_=psh.ap(), func=AF.Sigmoid)
            P.op("scalar", f_sig, reads=[psb], writes=[R_b])

            psh2, psb2 = ctx.ps[2 + do % 2]

            def f_mm_k(eng, do=do, psh2=psh2):
                last = None
                for dc in range(NC):
                    lhsT = w_k.ap()[:, dc * D + do * 128: dc * D + do * 128 + 128]
                    rhs = xk.ap().rearrange("p (c t) -> p c t", c=NC)[:, dc, :]
                    last = eng.matmul(psh2.ap(), lhsT, rhs, start=(dc == 0), stop=(dc == NC - 1))
                return last
            P.op("tensor", f_mm_k, reads=[xk_b, w_k_b], writes=[psb2])

            def f_relu(eng, do=do, psh2=psh2):
                o = KB.ap().rearrange("p (c t) -> p c t", c=NC)[:, do, :]
                return eng.activation(out=o, in_=psh2.ap(), func=AF.Relu)
            P.op("scalar", f_relu, reads=[psb2], writes=[KB_b])

            def f_sq(eng, do=do):
                o = K2.ap().rearrange("p (c t) -> p c t", c=NC)[:, do, :]
                i = KB.ap().rearrange("p (c t) -> p c t", c=NC)[:, do, :]
                return eng.tensor_tensor(o, i, i, ALU.mult)
            P.op("vector", f_sq, reads=[KB_b], writes=[K2_b])

        for do in range(NC):
            psh, psb = ctx.ps[4 + do % 2]

            def f_mm_v(eng, do=do, psh=psh):
                last = None
                for dc in range(NC):
                    lhsT = w_v.ap()[:, dc * D + do * 128: dc * D + do * 128 + 128]
                    rhs = K2.ap().rearrange("p (c t) -> p c t", c=NC)[:, dc, :]
                    last = eng.matmul(psh.ap(), lhsT, rhs, start=(dc == 0), stop=(dc == NC - 1))
                return last
            P.op("tensor", f_mm_v, reads=[K2_b, w_v_b], writes=[psb])

            def f_mul(eng, do=do, psh=psh):
                return eng.tensor_tensor(tmp.ap(), psh.ap(),
                                         R.ap().rearrange("p (c t) -> p c t", c=NC)[:, do, :],
                                         ALU.mult)
            P.op("vector", f_mul, reads=[psb, R_b], writes=[tmp_b])

            def f_add(eng, do=do, t0=t0):
                o = xt.ap().rearrange("p (c t) -> p c t", c=NC)[:, do, t0:t0 + 512]
                return eng.tensor_add(o, o, tmp.ap())
            P.op("vector", f_add, reads=[tmp_b, xt_b], writes=[xt_b])





def _b0_d0_q(qt, seg, prefix):
    return (2048 if prefix else 0) + 128 * qt - 512 * seg


def _b0_d0_t(kc, qtb):
    if kc < 16:
        return 2048 + 512 * qtb - 128 * kc
    return 512 * qtb - 128 * (kc - 16)


def _b0_key(d0):
    return 576 if d0 >= 576 else d0


def b0_used_keys():
    qk = set()
    for cls in (0, 1):
        for qt in range(16):
            for seg in range(4):
                qk.add((cls, _b0_key(_b0_d0_q(qt, seg, True))))
            for seg in range(qt // 4 + 1):
                qk.add((cls, _b0_key(_b0_d0_q(qt, seg, False))))
    tk = set()
    for cls in (0, 1):
        for qtb in range(4):
            for kc in list(range(16)) + [16 + c for c in range(4 * (qtb + 1))]:
                tk.add((cls, _b0_key(_b0_d0_t(kc, qtb))))
    return sorted(qk), sorted(tk)


B0Q_KEYS, B0T_KEYS = b0_used_keys()


def b0_variant_q(hp, qt, seg, prefix):
    cls = 0 if hp < 4 else 1
    return B0Q_KEYS.index((cls, _b0_key(_b0_d0_q(qt, seg, prefix))))


def b0_variant_t(hp, kc, qtb):
    cls = 0 if hp < 4 else 1
    return B0T_KEYS.index((cls, _b0_key(_b0_d0_t(kc, qtb))))


def _b0_allowed(cls, delta, jmod):
    causal = delta >= 0
    if cls == 0:
        return causal & ((delta < 64) | (delta % 64 == 0))
    sameblock = (delta >= 0) & (delta <= 63) & (jmod + np.minimum(delta, 63) <= 63 + 0)
    sameblock = causal & (jmod + delta < 64)
    return causal & (sameblock | (jmod == 63))


def b0_build_masks():
    import ml_dtypes
    qi = np.arange(128)[:, None]
    kj = np.arange(512)[None, :]
    mq = np.zeros((128, len(B0Q_KEYS) * 512), np.float32)
    for vi, (cls, d0) in enumerate(B0Q_KEYS):
        delta = d0 + qi - kj
        allowed = _b0_allowed(cls, delta, kj % 64)
        mq[:, vi * 512:(vi + 1) * 512] = np.where(allowed, 0.0, -1e9)
    kl = np.arange(128)[:, None]
    ql = np.arange(512)[None, :]
    mt = np.zeros((128, len(B0T_KEYS) * 512), np.float32)
    for vi, (cls, d0) in enumerate(B0T_KEYS):
        delta = d0 + ql - kl
        allowed = _b0_allowed(cls, delta, kl % 64)
        mt[:, vi * 512:(vi + 1) * 512] = np.where(allowed, 0.0, -1e9)
    return mq.astype(ml_dtypes.bfloat16), mt.astype(ml_dtypes.bfloat16)



COMMON_SB = [
    ("lnx2", [128, NC * 512], "BF16"),
    ("lnxb", [128, NC * 512], "BF16"),
    ("lnrow", [1, 2 * 512], "F32"),
    ("lntmp", [128, 512], "F32"),
    ("ones_colb", [128, 1], "BF16"),
    ("ones_row", [1, 128], "F32"),
    ("consts", [1, 8], "F32"),
]

KIND_SPECS = {
    "ffn0": dict(
        dram=[("w_w", [NC, 128, D], "BF16", "w_ffn0_w"),
              ("w_a1", [NC, 128, D], "BF16", "w_ffn0_a1"),
              ("w_a2", [NC, 128, D], "BF16", "w_ffn0_a2"),
              ("b_f", [128, NC], "F32", "b_ffn0")],
        sb=[("xln", [128, NC * TL], "BF16"),
            ("hb", [128, NC * TL], "BF16")],
        emit=lambda ctx: emit_ffn0(ctx),
    ),
    "attn_a0": dict(
        dram=[("xp", [D, TL], "F32", None),
              ("w_q", [NC, 128, D], "BF16", None),
              ("w_k", [NC, 128, D], "BF16", "w_a0_k"),
              ("w_v", [NC, 128, D], "BF16", "w_a0_v"),
              ("bq", [128, NC], "F32", "v_a0_bq"),
              ("bk", [128, NC], "F32", "v_a0_bk"),
              ("bvrow", [1, D], "F32", "v_a0_bv"),
              ("gam", [1, 2], "F32", "a0_gam"),
              ("identd", [128, 128], "BF16", "ident"),
              ("cmaskT", [128, 128], "BF16", "a0_cmask")],
        sb=[("a0_xlng", [128, NC * 512], "BF16"),
            ("a0_grp", [128, 24 * 512], "BF16"),
            ("a0_pre", [128, 16 * 512], "BF16"),
            ("a0_S", [128, NC * 1024], "BF16"),
            ("a0_z", [128, 16], "F32"),
            ("a0_zbf", [128, NC], "BF16"),
            ("a0_tab", [128, 1024], "F32"),
            ("a0_mdec", [128, 128], "BF16"),
            ("a0_qdecb", [128, 128], "BF16"),
            ("a0_kdec", [128, 2], "F32"),
            ("a0_wprec", [128, 16], "F32"),
            ("a0_bvbc", [128, 1024], "BF16"),
            ("a0_ktm", [128, 1024], "BF16")],
        emit=lambda ctx: emit_attn_a0(ctx, ctx.dram["xp"]),
    ),
    "attn_a1": dict(
        custom_x=True, custom_out=True,
        dram=[("x", [D, TL], "F32", None),
              ("xp", [D, TL], "F32", None),
              ("w_q", [NC, 128, D], "BF16", None),
              ("w_k", [NC, 128, D], "BF16", None),
              ("w_v", [NC, 128, D], "BF16", None),
              ("w_o", [NC, 128, D], "BF16", None),
              ("gqb", [128, 4], "F32", "a1_gqb"),
              ("bo", [128, NC], "F32", "v_a1_bo"),
              ("e64", [128, 2], "BF16", "a1_e64"),
              ("E2", [2, 128], "BF16", "a1_E2"),
              ("utri", [128, 128], "BF16", "a1_utri"),
              ("ut512", [128, 4 * 512], "BF16", "a1_ut512"),
              ("identd", [128, 128], "BF16", "ident"),
              ("pfl", [128, 1], "F32", "a1_pfl")],
        sb=[("lnsrc", [128, NC * 512], "F32"),
            ("a1_xln", [128, NC * TL], "BF16"),
            ("a1_xlnp", [128, NC * TL], "BF16"),
            ("a1_qt", [128, TL], "BF16"),
            ("a1_kt", [128, 2 * TL], "BF16"),
            ("a1_vf", [128, 512], "BF16"),
            ("a1_vtm", [128, 32 * 128], "BF16"),
            ("a1_khat", [128, 2 * TL], "BF16"),
            ("a1_qhat", [128, TL], "BF16"),
            ("a1_p", [128, 4 * 512], "BF16"),
            ("a1_ot", [128, NC * TL], "BF16"),
            ("a1_m", [128, 64], "F32"),
            ("a1_w", [128, NC * 1024], "BF16"),
            ("a1_ost", [128, 1024], "F32"),
            ("ceps", [128, 1], "F32")],
        emit=lambda ctx: emit_mha(ctx, ctx.dram["x"], ctx.dram["xp"]),
        big_psum=True,
    ),
    "attn_b0": dict(
        custom_x=True, custom_out=True,
        dram=[("x", [D, TL], "F32", None),
              ("xp", [D, TL], "F32", None),
              ("w_q", [NC, 128, D], "BF16", None),
              ("w_k", [NC, 128, D], "BF16", None),
              ("w_v", [NC, 128, D], "BF16", None),
              ("w_o", [NC, 128, D], "BF16", None),
              ("gqb", [128, 4], "F32", "a1_gqb"),
              ("bo", [128, NC], "F32", "v_a1_bo"),
              ("e64", [128, 2], "BF16", "a1_e64"),
              ("E2", [2, 128], "BF16", "a1_E2"),
              ("identd", [128, 128], "BF16", "ident"),
              ("b0mq", [128, len(B0Q_KEYS) * 512], "BF16", None),
              ("b0mt", [128, len(B0T_KEYS) * 512], "BF16", None),
              ("pfl", [128, 1], "F32", "a1_pfl")],
        sb=[("lnsrc", [128, NC * 512], "F32"),
            ("a1_xln", [128, NC * TL], "BF16"),
            ("a1_xlnp", [128, NC * TL], "BF16"),
            ("a1_qt", [128, TL], "BF16"),
            ("a1_kt", [128, 2 * TL], "BF16"),
            ("a1_vf", [128, 512], "BF16"),
            ("a1_vtm", [128, 32 * 128], "BF16"),
            ("a1_khat", [128, 2 * TL], "BF16"),
            ("a1_qhat", [128, TL], "BF16"),
            ("a1_p", [128, 4 * 512], "BF16"),
            ("a1_ot", [128, NC * TL], "BF16"),
            ("a1_m", [128, 64], "F32"),
            ("a1_ost", [128, 1024], "F32"),
            ("b0_m", [128, len(B0T_KEYS) * 512], "BF16"),
            ("ceps", [128, 1], "F32")],
        emit=lambda ctx: emit_mha(ctx, ctx.dram["x"], ctx.dram["xp"],
                                  head_ln=False, q_scale=0.125, mask_mode="b0",
                                  has_bo=False),
        big_psum=True,
    ),
    "attn_b1": dict(
        dram=[("x4", [D, W4_], "F32", None),
              ("wqk", [NC, 128, 32], "BF16", "w_b1_qk"),
              ("wv2", [NC, 128, D], "BF16", "w_b1_v"),
              ("wo2", [NC, 128, D], "BF16", "w_b1_o"),
              ("sel", [16, NC * 128], "BF16", "b1_sel"),
              ("kb", [16, 2], "F32", "b1_kb")],
        sb=[("lnsrc", [128, NC * 512], "F32"),
            ("b1_xln", [128, NC * 512], "BF16"),
            ("b1_sm", [16, 8 * 512], "BF16"),
            ("b1_th2", [16, 512], "BF16"),
            ("b1_kc2", [16, 2 * 512], "F32"),
            ("b1_vt", [128, 512], "BF16"),
            ("b1_scin", [128, 2 * 512], "BF16"),
            ("b1_cum", [128, 2 * 512], "F32"),
            ("b1_carry", [128, 20], "F32"),
            ("b1_zer", [128, 512], "BF16"),
            ("b1_heads", [128, NC * TL], "BF16"),
            ("b1_hh", [128, 2 * 512], "F32"),
            ("cpio2", [128, 1], "F32")],
        emit=lambda ctx: emit_attn_b1(ctx, ctx.dram["x4"]),
    ),
    "ffn1": dict(
        dram=[("w_r", [NC, 128, D], "BF16", "w_ffn1_r"),
              ("w_k", [NC, 128, D], "BF16", "w_ffn1_k"),
              ("w_v", [NC, 128, D], "BF16", "w_ffn1_v"),
              ("mu_r", [128, NC], "F32", "v_mu_r"),
              ("mu_k", [128, NC], "F32", "v_mu_k"),
              ("edge", [128, NC], "F32", "edge_sb")],
        sb=[("xln", [128, NC * TL], "BF16"),
            ("sg0", [128, NC * 512], "BF16"),
            ("sg1", [128, NC * 512], "BF16"),
            ("sg2", [128, NC * 512], "BF16"),
            ("sg3", [128, NC * 512], "BF16"),
            ("edge_ln", [128, NC], "BF16")],
        emit=lambda ctx: emit_ffn1(ctx),
    ),
}

_DT = {"F32": F32, "BF16": BF16, "I32": I32}


DEBUG_DUMP = []


def build_program(kinds):
    nc = bass.Bass("TRN2", target_bir_lowering=False, debug=False)
    dbg_out = nc.dram_tensor("dbg", [128, 2048], F32, kind="ExternalOutput") if DEBUG_DUMP else None
    custom_x = any(KIND_SPECS[k].get("custom_x") for k in kinds)
    custom_out = any(KIND_SPECS[k].get("custom_out") for k in kinds)
    big_psum = any(KIND_SPECS[k].get("big_psum") for k in kinds)
    if not custom_x:
        x_in = nc.dram_tensor("x", [D, TL], F32, kind="ExternalInput")
    y_out = nc.dram_tensor("y", [D, TL], F32, kind="ExternalOutput")
    dram = {}
    for kind in kinds:
        for (dname, shape, dt, sbname) in KIND_SPECS[kind]["dram"]:
            dram[dname] = (nc.dram_tensor(dname, shape, _DT[dt], kind="ExternalInput"), sbname, shape)

    import contextlib
    with contextlib.ExitStack() as st:
        prog = Prog(nc)
        ctx = Ctx(nc, prog)
        ctx.y_out = y_out
        if big_psum:
            h = st.enter_context(nc.psum_tensor("bigps", [128, 2048], F32))
            ctx.big_ps = (h, Buf("bigps"))
            ctx.alloc_psum(st, 4)
            ctx.ln_banks = (2, 3)
        else:
            ctx.alloc_psum(st, 8)
        if not custom_x:
            ctx.alloc_sb(st, "xt", [128, NC * TL], F32)
        for (name, shape, dt) in COMMON_SB:
            ctx.alloc_sb(st, name, shape, _DT[dt])
        if DEBUG_DUMP:
            ctx.alloc_sb(st, "dbgstg", [128, 512], F32)
        done = set()
        for kind in kinds:
            for (name, shape, dt) in KIND_SPECS[kind]["sb"]:
                if name not in done:
                    ctx.alloc_sb(st, name, shape, _DT[dt])
                    done.add(name)
            for (dname, shape, dt, sbname) in KIND_SPECS[kind]["dram"]:
                if sbname is not None and sbname not in done:
                    if len(shape) == 3:
                        ctx.alloc_sb(st, sbname, [shape[1], shape[0] * shape[2]], _DT[dt])
                    else:
                        ctx.alloc_sb(st, sbname, shape, _DT[dt])
                    done.add(sbname)
        for dname, (dh, sbname, shape) in dram.items():
            ctx.dram[dname] = dh

        P = prog
        xt = ctx.sb["xt"][0] if "xt" in ctx.sb else None
        oc = ctx.sb["ones_colb"][0]
        orow = ctx.sb["ones_row"][0]
        cstt = ctx.sb["consts"][0]
        P.op("vector", lambda e: e.memset(oc.ap(), 1.0), writes=[ctx.sb["ones_colb"][1]])
        P.op("vector", lambda e: e.memset(orow.ap(), 1.0), writes=[ctx.sb["ones_row"][1]])

        def f_csts(e):
            e.memset(cstt.ap()[0:1, 0:1], 1e-5)
            e.memset(cstt.ap()[0:1, 2:3], -1.0)
            return e.memset(cstt.ap()[0:1, 1:2], 1e-6)
        P.op("vector", f_csts, writes=[ctx.sb["consts"][1]])

        if not custom_x:
            P.op("sync", lambda e: e.dma_start(out=xt.ap().rearrange("p (c t) -> p c t", c=NC),
                                               in_=x_in.ap().rearrange("(c p) t -> p c t", p=128)),
                 writes=[ctx.sb["xt"][1]], dma=True)
        if "ceps" in ctx.sb:
            P.op("vector", lambda e: e.memset(ctx.sb["ceps"][0].ap(), 1e-5),
                 writes=[ctx.sb["ceps"][1]])
        if "cpio2" in ctx.sb:
            P.op("vector", lambda e: e.memset(ctx.sb["cpio2"][0].ap(), PI / 2),
                 writes=[ctx.sb["cpio2"][1]])
            P.op("vector", lambda e: e.memset(ctx.sb["b1_zer"][0].ap(), 0.0),
                 writes=[ctx.sb["b1_zer"][1]])
        for dname, (dh, sbname, shape) in dram.items():
            if sbname is None:
                continue
            h, bfr = ctx.sb[sbname]

            def f_load(e, dh=dh, h=h, shape=shape):
                if len(shape) == 3:
                    return e.dma_start(out=h.ap().rearrange("p (c d) -> p c d", c=shape[0]),
                                       in_=dh.ap().transpose([1, 0, 2]))
                return e.dma_start(out=h.ap(), in_=dh.ap())
            P.op("sync", f_load, writes=[bfr], dma=True)

        for kind in kinds:
            KIND_SPECS[kind]["emit"](ctx)

        if custom_out:
            waits = [("gD", prog.plan["gD"], "gpsimd", True)]
        else:
            t_store = P.op("gpsimd", lambda e: e.dma_start(
                out=y_out.ap().rearrange("(c p) t -> p c t", p=128),
                in_=xt.ap().rearrange("p (c t) -> p c t", c=NC)),
                reads=[ctx.sb["xt"][1]], dma=True)
            waits = [t_store]
        if DEBUG_DUMP:
            dcol = 0
            for ent in DEBUG_DUMP:
                (name, scol, w, bc) = ent[:4]
                rows = ent[4] if len(ent) > 4 else 128
                if name.startswith("PS:"):
                    h, bfr = ctx.ps[int(name[3:])]
                    ap = h.ap()
                else:
                    h, bfr = ctx.sb[name]
                    ap = h.ap().bitcast(F32) if bc == "f32" else (h.ap().bitcast(BF16) if bc == "bf16" else h.ap())
                src = ap[0:rows, scol:scol + w]
                stg, stg_b = ctx.sb["dbgstg"]
                def f_cp(e, src=src, w=w, rows=rows):
                    return e.tensor_copy(stg.ap()[0:rows, 0:w], src)
                tcp = P.op("vector", f_cp, reads=[bfr], writes=[stg_b])
                def f_dma(e, dcol=dcol, w=w, rows=rows):
                    return e.dma_start(out=dbg_out.ap()[0:rows, dcol:dcol + w], in_=stg.ap()[0:rows, 0:w])
                waits.append(P.op("gpsimd", f_dma, reads=[stg_b], dma=True))
                dcol += w
        P.op("gpsimd", lambda e: e.engine_nop(), after=waits)

        build_and_finalize(nc, prog)
    return nc


_PROG_CACHE = {}


def _prog(kind):
    if kind not in _PROG_CACHE:
        _PROG_CACHE[kind] = build_program([kind])
    return _PROG_CACHE[kind]


DBG_RANGE = (0, 8)
DBG_X0 = None
DBG_TRACE = None
DBG_KEEP = None


def kernel(**inputs):
    import ml_dtypes
    f32 = np.float32
    g = lambda n: np.asarray(inputs[n], f32)
    _ = inputs.get("L1_attn_a_mask")

    ident = np.eye(128, dtype=ml_dtypes.bfloat16)
    r128 = np.arange(128)
    cmaskT = (r128[None, :] >= r128[:, None]).astype(ml_dtypes.bfloat16)
    utri = np.where(r128[:, None] > r128[None, :], -1e9, 0.0).astype(ml_dtypes.bfloat16)
    ut512 = np.zeros((128, 4 * 512), f32)
    i512 = np.arange(512)
    for j0 in range(4):
        ut512[:, j0 * 512:(j0 + 1) * 512] = (r128[:, None] > (i512[None, :] - j0 * 128)) * -1e9
    ut512 = ut512.astype(ml_dtypes.bfloat16)
    e64 = np.zeros((128, 2), f32); e64[0:64, 0] = 1; e64[64:128, 1] = 1
    e64 = e64.astype(ml_dtypes.bfloat16)
    E2 = np.zeros((2, 128), f32); E2[0, 0:64] = 1; E2[1, 64:128] = 1
    E2 = E2.astype(ml_dtypes.bfloat16)
    selmat = np.zeros((16, 8 * 128), f32)
    for cc in range(8):
        for p in range(128):
            selmat[2 * cc + (p >= 64), cc * 128 + p] = 1.0
    selb = selmat.astype(ml_dtypes.bfloat16)
    mq, mt = b0_build_masks()

    gamma = float(np.asarray(inputs["L0_attn_a_gamma"]))

    wa0 = dict(w_q=_wmat(g("L0_attn_a_Wq").T), w_k=_wmat(g("L0_attn_a_Wk").T),
               w_v=_wmat(g("L0_attn_a_Wv").T),
               bq=_wvec(g("L0_attn_a_bq")), bk=_wvec(g("L0_attn_a_bk")),
               bvrow=np.ascontiguousarray(g("L0_attn_a_bv").reshape(1, D)),
               identd=ident, cmaskT=cmaskT)
    wf0 = dict(w_w=_wmat(g("L0_ffn_weight")), w_a1=_wmat(g("L0_ffn_aux_weight1")),
               w_a2=_wmat(g("L0_ffn_aux_weight2")), b_f=_wvec(g("L0_ffn_bias")))
    wb0 = dict(w_q=_wmat(g("L0_attn_b_Wq").T), w_k=_wmat(g("L0_attn_b_Wk").T),
               w_v=_wmat(g("L0_attn_b_Wv").T), w_o=_wmat(g("L0_attn_b_Wo").T),
               gqb=np.zeros((128, 4), f32), bo=_wvec(np.zeros(D, f32)),
               e64=e64, E2=E2, identd=ident, b0mq=mq, b0mt=mt)
    gq, bq1, gk, bk1 = (g("L1_attn_a_gq"), g("L1_attn_a_bq"),
                        g("L1_attn_a_gk"), g("L1_attn_a_bk"))
    gqb = np.stack([np.tile(gq, 2), np.tile(bq1, 2), np.tile(gk, 2), np.tile(bk1, 2)],
                   axis=1).astype(f32)
    wa1 = dict(w_q=_wmat(g("L1_attn_a_Wq").T), w_k=_wmat(g("L1_attn_a_Wk").T),
               w_v=_wmat(g("L1_attn_a_Wv").T), w_o=_wmat(g("L1_attn_a_Wo").T),
               gqb=gqb, bo=_wvec(g("L1_attn_a_bo")),
               e64=e64, E2=E2, utri=utri, ut512=ut512, identd=ident)
    wf1 = dict(w_r=_wmat(g("L1_ffn_W_r").T), w_k=_wmat(g("L1_ffn_W_k").T),
               w_v=_wmat(g("L1_ffn_W_v").T),
               mu_r=_wvec(g("L1_ffn_mu_r")), mu_k=_wvec(g("L1_ffn_mu_k")))
    WQ2 = g("L1_attn_b_W_Q")[:, :, 0].T
    WK2 = g("L1_attn_b_W_K")[:, :, 0].T
    WV2 = np.ascontiguousarray(g("L1_attn_b_W_V").transpose(1, 0, 2).reshape(D, D))
    WO2 = np.ascontiguousarray(g("L1_attn_b_W_O").transpose(2, 1, 0).reshape(D, D))
    wb1 = dict(wqk=_wmat(np.concatenate([WQ2, WK2], axis=1)),
               wv2=_wmat(WV2), wo2=_wmat(WO2), sel=selb)

    cur = np.ascontiguousarray(g("L0_attn_a_x_seq"))

    def shard_T(a):
        return [np.ascontiguousarray(a[c // 2, (c % 2) * TL:(c % 2 + 1) * TL].T)
                for c in range(8)]

    def prefix_T(a):
        out = []
        for c in range(8):
            b, hh = c // 2, c % 2
            xp = np.zeros((TL, D), f32) if hh == 0 else a[b, 0:TL]
            out.append(np.ascontiguousarray(xp.T))
        return out

    def gather(res):
        out = np.zeros((B, T, D), f32)
        for c, r in enumerate(res.results):
            b, hh = c // 2, c % 2
            out[b, hh * TL:(hh + 1) * TL] = r["y"].T
        return out

    def launch(kind, per_core):
        nc = _prog(kind)
        res = run_bass_kernel_spmd(nc, per_core, core_ids=list(range(8)))
        if DBG_KEEP is not None:
            DBG_KEEP.append((kind, res))
        return gather(res)

    def st_attn_a0(cur):
        pc = []
        xs, xps = shard_T(cur), prefix_T(cur)
        for c in range(8):
            hh = c % 2
            d = dict(wa0)
            d["x"] = xs[c]; d["xp"] = xps[c]
            d["gam"] = np.array([[gamma, 1.0 if hh == 1 else 0.0]], f32)
            pc.append(d)
        return launch("attn_a0", pc)

    def st_ffn0(cur):
        xs = shard_T(cur)
        pc = [dict(wf0, x=xs[c]) for c in range(8)]
        return launch("ffn0", pc)

    def st_attn_b0(cur):
        xs, xps = shard_T(cur), prefix_T(cur)
        pc = []
        for c in range(8):
            hh = c % 2
            d = dict(wb0)
            d["x"] = xs[c]; d["xp"] = xps[c]
            d["pfl"] = np.full((128, 1), 1.0 if hh == 1 else 0.0, f32)
            pc.append(d)
        return launch("attn_b0", pc)

    def st_attn_a1(cur):
        xs, xps = shard_T(cur), prefix_T(cur)
        pc = []
        for c in range(8):
            hh = c % 2
            d = dict(wa1)
            d["x"] = xs[c]; d["xp"] = xps[c]
            d["pfl"] = np.full((128, 1), 1.0 if hh == 1 else 0.0, f32)
            pc.append(d)
        return launch("attn_a1", pc)

    def st_ffn1(cur):
        xs = shard_T(cur)
        pc = []
        for c in range(8):
            b, hh = c // 2, c % 2
            edge = np.zeros(D, f32) if hh == 0 else cur[b, TL - 1]
            pc.append(dict(wf1, x=xs[c], edge=_wvec(edge)))
        return launch("ffn1", pc)

    def st_attn_b1(cur):
        pc = []
        for c in range(8):
            b, hh = c // 2, c % 2
            own = cur[b, hh * TL:(hh + 1) * TL]
            if hh == 0:
                win = np.concatenate([np.zeros((TL, D), f32), own], axis=0)
                kb = np.zeros((16, 2), f32); kb[:, 0] = -2048.0
            else:
                win = cur[b]
                kb = np.zeros((16, 2), f32)
            d = dict(wb1)
            d["x"] = np.ascontiguousarray(own.T)
            d["x4"] = np.ascontiguousarray(win.T)
            d["kb"] = kb
            pc.append(d)
        return launch("attn_b1", pc)

    stages = [st_attn_a0, st_ffn0, st_attn_b0, st_ffn0,
              st_attn_a1, st_ffn1, st_attn_b1, st_ffn1]
    if DBG_X0 is not None:
        cur = np.ascontiguousarray(np.asarray(DBG_X0, f32).astype(f32))
    for f in stages[DBG_RANGE[0]:DBG_RANGE[1]]:
        cur = f(cur)
        if DBG_TRACE is not None:
            DBG_TRACE.append(cur.copy())

    return cur



W4 = 4096


def emit_attn_b1(ctx, x4_dram):
    nc, P = ctx.nc, ctx.P
    xt, xt_b = ctx.get("xt")
    lnsrc, lnsrc_b = ctx.get("lnsrc")
    xln_s, xln_s_b = ctx.get("b1_xln")
    sm, _ = ctx.get("b1_sm")
    ck_b, sk_b, cq_b, sq_b, kc_b, ks_b, denr_b, th_b = (Buf(n) for n in
        ("ck", "sk", "cq", "sq", "kc", "ks", "denr", "th"))

    def sm_ap(slot, w=512):
        return sm.ap()[0:16, slot * 512:slot * 512 + w]
    th2, th2_b = ctx.get("b1_th2")
    kc2, _ = ctx.get("b1_kc2")

    vt_s, vt_b = ctx.get("b1_vt")
    scin, scin_b = ctx.get("b1_scin")
    cum, cum_b = ctx.get("b1_cum")
    carry, carry_b = ctx.get("b1_carry")
    zer, zer_b = ctx.get("b1_zer")
    heads, heads_b = ctx.get("b1_heads")
    hh, hh_b = ctx.get("b1_hh")
    wqk, wqk_b = ctx.get("w_b1_qk")
    wv, wv_b = ctx.get("w_b1_v")
    wo, wo_b = ctx.get("w_b1_o")
    sel, sel_b = ctx.get("b1_sel")
    kb, kb_b = ctx.get("b1_kb")
    cpio2, cpio2_b = ctx.get("cpio2")
    tmp, tmp_b = ctx.get("lntmp")

    for tg in range(8):
        own = tg >= 4
        o0 = (tg - 4) * 512

        def f_ldx(e, tg=tg):
            return e.dma_start(out=lnsrc.ap().rearrange("p (c t) -> p c t", c=NC),
                               in_=x4_dram.ap().rearrange("(c p) t -> p c t", p=128)[:, :, tg * 512:(tg + 1) * 512])
        P.op("sync", f_ldx, writes=[lnsrc_b], dma=True)
        emit_ln_tg(ctx, lnsrc, lnsrc_b, NC, 0, xln_s, xln_s_b, NC, 0)

        def xls(dc):
            return xln_s.ap().rearrange("p (c t) -> p c t", c=NC)[:, dc, :]

        psq, psq_b = ctx.ps[0]
        psk, psk_b = ctx.ps[1]

        def f_qk(eng, tg=tg):
            last = None
            for dc in range(NC):
                last = eng.matmul(psq.ap()[0:16, :], wqk.ap()[:, dc * 32:dc * 32 + 16],
                                  xls(dc), start=(dc == 0), stop=(dc == NC - 1))
            for dc in range(NC):
                last = eng.matmul(psk.ap()[0:16, :], wqk.ap()[:, dc * 32 + 16:dc * 32 + 32],
                                  xls(dc), start=(dc == 0), stop=(dc == NC - 1))
            return last
        P.op("tensor", f_qk, reads=[xln_s_b, wqk_b], writes=[psq_b, psk_b])

        def f_tanh(eng):
            eng.activation(out=sm_ap(7), in_=psq.ap()[0:16, :], func=AF.Tanh)
            return eng.activation(out=th2.ap(), in_=psk.ap()[0:16, :], func=AF.Tanh)
        P.op("scalar", f_tanh, reads=[psq_b, psk_b], writes=[th_b, th2_b])

        def f_sins(eng, own=own):
            eng.activation(out=sm_ap(0), in_=th2.ap(), func=AF.Sin,
                           scale=PI / 4, bias=cpio2.ap()[0:16, 0:1])
            last = eng.activation(out=sm_ap(1), in_=th2.ap(), func=AF.Sin, scale=PI / 4)
            if own:
                eng.activation(out=sm_ap(2), in_=sm_ap(7), func=AF.Sin,
                               scale=PI / 4, bias=cpio2.ap()[0:16, 0:1])
                last = eng.activation(out=sm_ap(3), in_=sm_ap(7), func=AF.Sin, scale=PI / 4)
            return last
        P.op("scalar", f_sins, reads=[th_b, th2_b, cpio2_b],
             writes=[ck_b, sk_b, cq_b, sq_b])

        def f_kcum(eng, tg=tg):
            ini_c = kb.ap()[0:16, 0:1] if tg == 0 else carry.ap()[0:16, 16:17]
            ini_s = kb.ap()[0:16, 1:2] if tg == 0 else carry.ap()[0:16, 17:18]
            eng.tensor_tensor_scan(kc2.ap()[0:16, 0:512], sm_ap(0), zer.ap()[0:16, 0:512],
                                   ini_c, ALU.add, ALU.add)
            eng.tensor_tensor_scan(kc2.ap()[0:16, 512:1024], sm_ap(1), zer.ap()[0:16, 0:512],
                                   ini_s, ALU.add, ALU.add)
            eng.tensor_copy(carry.ap()[0:16, 16:17], kc2.ap()[0:16, 511:512])
            return eng.tensor_copy(carry.ap()[0:16, 17:18], kc2.ap()[0:16, 1023:1024])
        P.op("vector", f_kcum, reads=[ck_b, sk_b, kb_b, zer_b, carry_b],
             writes=[kc_b, ks_b, carry_b])

        if own:
            def f_den(eng):
                eng.tensor_tensor(tmp.ap()[0:16, :], sm_ap(2), kc2.ap()[0:16, 0:512], ALU.mult)
                eng.tensor_tensor(hh.ap()[0:16, 0:512], sm_ap(3), kc2.ap()[0:16, 512:1024], ALU.mult)
                eng.tensor_add(tmp.ap()[0:16, :], tmp.ap()[0:16, :], hh.ap()[0:16, 0:512])
                eng.tensor_scalar_add(tmp.ap()[0:16, :], tmp.ap()[0:16, :], 1e-6)
                with ctx.nc.allow_low_precision(reason="bf16 store of reciprocal"):
                    return eng.reciprocal(sm_ap(6), tmp.ap()[0:16, :])
            P.op("vector", f_den, reads=[cq_b, sq_b, kc_b, ks_b],
                 writes=[denr_b, tmp_b, hh_b])

        for cc in range(NC):
            psv, psv_b = ctx.ps[cc % 2]

            def f_vmm(eng, cc=cc, psv=psv):
                last = None
                for dc in range(NC):
                    lhsT = wv.ap()[:, dc * D + cc * 128: dc * D + cc * 128 + 128]
                    last = eng.matmul(psv.ap(), lhsT, xls(dc),
                                      start=(dc == 0), stop=(dc == NC - 1))
                return last
            P.op("tensor", f_vmm, reads=[xln_s_b, wv_b], writes=[psv_b])

            def f_vev(eng, psv=psv):
                return eng.tensor_copy(vt_s.ap(), psv.ap())
            P.op("vector", f_vev, reads=[psv_b], writes=[vt_b])

            psc, psc_b = ctx.ps[2]
            pss, pss_b = ctx.ps[3]

            def f_bc(eng, cc=cc):
                eng.matmul(psc.ap(), sel.ap()[:, cc * 128:cc * 128 + 128],
                           sm_ap(0), start=True, stop=True)
                return eng.matmul(pss.ap(), sel.ap()[:, cc * 128:cc * 128 + 128],
                                  sm_ap(1), start=True, stop=True)
            P.op("tensor", f_bc, reads=[sel_b, ck_b, sk_b], writes=[psc_b, pss_b])

            def f_sc(eng):
                eng.tensor_tensor(scin.ap()[:, 0:512], vt_s.ap(), psc.ap(), ALU.mult)
                return eng.tensor_tensor(scin.ap()[:, 512:1024], vt_s.ap(), pss.ap(), ALU.mult)
            P.op("vector", f_sc, reads=[vt_b, psc_b, pss_b], writes=[scin_b])

            def f_scan(eng, cc=cc, tg=tg):
                ini_c = 0.0 if tg == 0 else carry.ap()[:, 2 * cc:2 * cc + 1]
                ini_s = 0.0 if tg == 0 else carry.ap()[:, 2 * cc + 1:2 * cc + 2]
                eng.tensor_tensor_scan(cum.ap()[:, 0:512], scin.ap()[:, 0:512],
                                       zer.ap()[:, 0:512], ini_c, ALU.add, ALU.add)
                eng.tensor_tensor_scan(cum.ap()[:, 512:1024], scin.ap()[:, 512:1024],
                                       zer.ap()[:, 0:512], ini_s, ALU.add, ALU.add)
                eng.tensor_copy(carry.ap()[:, 2 * cc:2 * cc + 1], cum.ap()[:, 511:512])
                return eng.tensor_copy(carry.ap()[:, 2 * cc + 1:2 * cc + 2], cum.ap()[:, 1023:1024])
            P.op("vector", f_scan, reads=[scin_b, zer_b, carry_b], writes=[cum_b, carry_b])

            if own:
                psq2, psq2_b = ctx.ps[2]
                pss2, pss2_b = ctx.ps[3]
                psr, psr_b = ctx.ps[4]

                def f_bc2(eng, cc=cc):
                    eng.matmul(psq2.ap(), sel.ap()[:, cc * 128:cc * 128 + 128],
                               sm_ap(2), start=True, stop=True)
                    eng.matmul(pss2.ap(), sel.ap()[:, cc * 128:cc * 128 + 128],
                               sm_ap(3), start=True, stop=True)
                    return eng.matmul(psr.ap(), sel.ap()[:, cc * 128:cc * 128 + 128],
                                      sm_ap(6), start=True, stop=True)
                P.op("tensor", f_bc2, reads=[sel_b, cq_b, sq_b, denr_b],
                     writes=[psq2_b, pss2_b, psr_b])

                def f_h(eng, cc=cc, o0=o0):
                    eng.tensor_tensor(hh.ap()[:, 0:512], cum.ap()[:, 0:512], psq2.ap(), ALU.mult)
                    eng.tensor_tensor(hh.ap()[:, 512:1024], cum.ap()[:, 512:1024], pss2.ap(), ALU.mult)
                    eng.tensor_add(hh.ap()[:, 0:512], hh.ap()[:, 0:512], hh.ap()[:, 512:1024])
                    o = heads.ap().rearrange("p (c t) -> p c t", c=NC)[:, cc, o0:o0 + 512]
                    return eng.tensor_tensor(o, hh.ap()[:, 0:512], psr.ap(), ALU.mult)
                P.op("vector", f_h, reads=[cum_b, psq2_b, pss2_b, psr_b],
                     writes=[hh_b, heads_b])

    for do in range(NC):
        for tgo in range(4):
            o0 = tgo * 512
            psh, psb = ctx.ps[(do * 4 + tgo) % 2]

            def f_omm(eng, do=do, o0=o0, psh=psh):
                last = None
                for cc in range(NC):
                    lhsT = wo.ap()[:, cc * D + do * 128: cc * D + do * 128 + 128]
                    rhs = heads.ap().rearrange("p (c t) -> p c t", c=NC)[:, cc, o0:o0 + 512]
                    last = eng.matmul(psh.ap(), lhsT, rhs, start=(cc == 0), stop=(cc == NC - 1))
                return last
            P.op("tensor", f_omm, reads=[heads_b, wo_b], writes=[psb])

            def f_add(eng, do=do, o0=o0, psh=psh):
                o = xt.ap().rearrange("p (c t) -> p c t", c=NC)[:, do, o0:o0 + 512]
                return eng.tensor_add(o, o, psh.ap())
            P.op("vector", f_add, reads=[psb, xt_b], writes=[xt_b])



CA0 = 128


def emit_attn_a0(ctx, xp_dram):
    nc, P = ctx.nc, ctx.P
    xt, xt_b = ctx.get("xt")
    xlng, xlng_b = ctx.get("a0_xlng")
    grp, grp_shared_b = ctx.get("a0_grp")
    lnsrc, lnsrc_b = grp, grp_shared_b
    pre, pre_b0 = ctx.get("a0_pre")
    S, S_b = ctx.get("a0_S")
    z, z_b = ctx.get("a0_z")
    zbf, zbf_b = ctx.get("a0_zbf")
    tab, tab_b = ctx.get("a0_tab")
    mdec, mdec_b = ctx.get("a0_mdec")
    qdecb, qdecb_b = ctx.get("a0_qdecb")
    kdec, kdec_b = ctx.get("a0_kdec")
    wprec, wprec_b = ctx.get("a0_wprec")
    bv_bc, bv_bc_b = ctx.get("a0_bvbc")
    ident, ident_b = ctx.get("ident")
    cmask, cmask_b = ctx.get("a0_cmask")
    gam, gam_b = ctx.get("a0_gam")
    ktm, ktm_b = ctx.get("a0_ktm")
    wk, wk_b = ctx.get("w_a0_k")
    wv, wv_b = ctx.get("w_a0_v")
    bq, bq_b = ctx.get("v_a0_bq")
    bk, bk_b = ctx.get("v_a0_bk")
    bvr, bvr_b = ctx.get("v_a0_bv")
    ones_row, _ = ctx.get("ones_row")
    ones_colb, _ = ctx.get("ones_colb")
    tmp, tmp_b = ctx.get("lntmp")
    EPS = 1e-6
    ALP = ctx.nc.allow_low_precision

    pre_b = pre_b0

    def lnsrc_ap():
        return grp.ap().bitcast(F32)[:, 0:NC * 512]

    def kfm_ap(dc):
        return grp.ap()[:, 8192 + dc * 128: 8192 + dc * 128 + 128]
    kfm_b = grp_shared_b

    psA, psA_b = ctx.ps[6]
    psB, psB_b = ctx.ps[7]

    def f_abs(eng):
        return eng.activation(out=tab.ap()[0:1, 0:1], in_=gam.ap()[0:1, 0:1], func=AF.Abs)
    P.op("scalar", f_abs, reads=[gam_b], writes=[tab_b])

    def f_lng(eng):
        eng.activation(out=tab.ap()[0:1, 1:2], in_=tab.ap()[0:1, 0:1], func=AF.Ln)
        eng.activation(out=tab.ap()[0:1, 2:3], in_=tab.ap()[0:1, 1:2], func=AF.Exp, scale=float(CA0))
        return eng.sign(tab.ap()[0:1, 3:4], gam.ap()[0:1, 0:1])
    P.op("scalar", f_lng, reads=[tab_b, gam_b], writes=[tab_b])

    P.op("vector", lambda e: e.memset(tab.ap()[0:1, 4:5], 1.0), writes=[tab_b])

    def f_bc_scal(eng):
        return eng.matmul(psA.ap()[:, 0:4], ones_row.ap(), tab.ap()[0:1, 0:4],
                          start=True, stop=True)
    P.op("tensor", f_bc_scal, reads=[tab_b], writes=[psA_b])

    def f_cp_scal(eng):
        e = eng
        e.copy(tab.ap()[:, 8:9], psA.ap()[:, 1:2])
        e.copy(tab.ap()[:, 9:10], psA.ap()[:, 2:3])
        e.copy(kdec.ap()[:, 1:2], psA.ap()[:, 2:3])
        return e.copy(tab.ap()[:, 10:11], psA.ap()[:, 3:4])
    P.op("scalar", f_cp_scal, reads=[psA_b], writes=[tab_b, kdec_b])

    def f_par(eng):
        return eng.iota(tab.ap()[0:1, 640:768].rearrange("p (a b) -> p a b", a=64),
                        [[0, 64], [1, 2]], base=0, channel_multiplier=0, allow_small_or_imprecise_dtypes=True)
    P.op("gpsimd", f_par, writes=[tab_b])

    cst0, cst0_b = ctx.get("consts")

    def f_sgnm1(eng):
        return eng.activation(out=tab.ap()[0:1, 5:6], in_=tab.ap()[0:1, 3:4],
                              func=AF.Copy, bias=-1.0)
    P.op("scalar", f_sgnm1, reads=[tab_b, cst0_b], writes=[tab_b])

    def f_sgnrows(eng):
        eng.tensor_scalar(tab.ap()[0:1, 768:896], tab.ap()[0:1, 640:768],
                          tab.ap()[0:1, 5:6], 1.0, ALU.mult, ALU.add)
        eng.tensor_scalar(tab.ap()[0:1, 896:1024], tab.ap()[0:1, 640:768],
                          -1.0, 1.0, ALU.mult, ALU.add)
        return eng.tensor_scalar(tab.ap()[0:1, 896:1024], tab.ap()[0:1, 896:1024],
                                 tab.ap()[0:1, 5:6], 1.0, ALU.mult, ALU.add)
    P.op("vector", f_sgnrows, reads=[tab_b], writes=[tab_b])

    def f_sgncol(eng):
        return eng.matmul(psA.ap()[:, 4:5], tab.ap()[0:1, 896:1024],
                          tab.ap()[0:1, 4:5], start=True, stop=True)
    P.op("tensor", f_sgncol, reads=[tab_b], writes=[psA_b])

    def f_sgncol_cp(eng):
        return eng.tensor_copy(tab.ap()[:, 11:12], psA.ap()[:, 4:5])
    P.op("vector", f_sgncol_cp, reads=[psA_b], writes=[tab_b])

    def f_mdiota(eng):
        return eng.iota(tab.ap()[:, 256:384], [[1, 128]], base=0, channel_multiplier=-1, allow_small_or_imprecise_dtypes=True)
    P.op("gpsimd", f_mdiota, writes=[tab_b])

    def f_mdt(eng):
        return eng.tensor_scalar_mul(tab.ap()[:, 256:384], tab.ap()[:, 256:384],
                                     tab.ap()[:, 8:9])
    P.op("vector", f_mdt, reads=[tab_b], writes=[tab_b])

    def f_mde(eng):
        return eng.activation(out=tab.ap()[:, 256:384], in_=tab.ap()[:, 256:384], func=AF.Exp)
    P.op("scalar", f_mde, reads=[tab_b], writes=[tab_b])

    def f_sgnouter(eng):
        return eng.matmul(psA.ap()[:, 128:256], tab.ap()[0:1, 768:896],
                          tab.ap()[0:1, 768:896], start=True, stop=True)
    P.op("tensor", f_sgnouter, reads=[tab_b], writes=[psA_b])

    def f_mdec(eng):
        eng.tensor_tensor(tab.ap()[:, 384:512], tab.ap()[:, 256:384],
                          psA.ap()[:, 128:256], ALU.mult)
        with ALP(reason="bf16 decay mask"):
            return eng.tensor_tensor(mdec.ap(), tab.ap()[:, 384:512], cmask.ap(), ALU.mult)
    P.op("vector", f_mdec, reads=[tab_b, psA_b, cmask_b], writes=[mdec_b, tab_b])

    def f_kdiota(eng):
        return eng.iota(tab.ap()[:, 12:13], [[0, 1]], base=CA0 - 1, channel_multiplier=-1, allow_small_or_imprecise_dtypes=True)
    P.op("gpsimd", f_kdiota, writes=[tab_b])

    def f_kd1(eng):
        return eng.tensor_scalar_mul(tab.ap()[:, 12:13], tab.ap()[:, 12:13], tab.ap()[:, 8:9])
    P.op("vector", f_kd1, reads=[tab_b], writes=[tab_b])

    def f_kd2(eng):
        return eng.activation(out=tab.ap()[:, 13:14], in_=tab.ap()[:, 12:13], func=AF.Exp)
    P.op("scalar", f_kd2, reads=[tab_b], writes=[tab_b])

    def f_kd3(eng):
        return eng.tensor_tensor(tab.ap()[:, 14:15], tab.ap()[:, 13:14], tab.ap()[:, 11:12], ALU.mult)
    P.op("vector", f_kd3, reads=[tab_b], writes=[tab_b])

    def f_kd4(eng):
        return eng.copy(kdec.ap()[:, 0:1], tab.ap()[:, 14:15])
    P.op("scalar", f_kd4, reads=[tab_b], writes=[kdec_b])

    def f_qiota(eng):
        return eng.iota(tab.ap()[0:1, 128:256], [[1, 128]], base=1, channel_multiplier=0, allow_small_or_imprecise_dtypes=True)
    P.op("gpsimd", f_qiota, writes=[tab_b])

    def f_qd1(eng):
        return eng.tensor_scalar_mul(tab.ap()[0:1, 128:256], tab.ap()[0:1, 128:256],
                                     tab.ap()[0:1, 1:2])
    P.op("vector", f_qd1, reads=[tab_b], writes=[tab_b])

    def f_qd2(eng):
        return eng.activation(out=tab.ap()[0:1, 128:256], in_=tab.ap()[0:1, 128:256], func=AF.Exp)
    P.op("scalar", f_qd2, reads=[tab_b], writes=[tab_b])

    def f_qd3(eng):
        return eng.tensor_tensor(tab.ap()[0:1, 128:256], tab.ap()[0:1, 128:256],
                                 tab.ap()[0:1, 896:1024], ALU.mult)
    P.op("vector", f_qd3, reads=[tab_b], writes=[tab_b])

    def f_qdbc(eng):
        return eng.matmul(psA.ap()[:, 128:256], ones_row.ap(), tab.ap()[0:1, 128:256],
                          start=True, stop=True)
    P.op("tensor", f_qdbc, reads=[tab_b], writes=[psA_b])

    def f_qdcp(eng):
        with ALP(reason="bf16 decay table"):
            return eng.tensor_copy(qdecb.ap(), psA.ap()[:, 128:256])
    P.op("vector", f_qdcp, reads=[psA_b], writes=[qdecb_b])

    def f_wiota(eng):
        return eng.iota(lnsrc_ap()[0:1, 0:TL], [[-1, TL]], base=TL - 1, channel_multiplier=0, allow_small_or_imprecise_dtypes=True)
    P.op("gpsimd", f_wiota, writes=[lnsrc_b])

    def f_wpar(eng):
        return eng.iota(lnsrc_ap()[0:1, TL:2 * TL].rearrange("p (a b) -> p a b", a=TL // 2),
                        [[0, TL // 2], [1, 2]], base=1, channel_multiplier=0, allow_small_or_imprecise_dtypes=True)
    P.op("gpsimd", f_wpar, writes=[lnsrc_b])

    def f_wp1(eng):
        eng.tensor_scalar(lnsrc_ap()[0:1, TL:2 * TL], lnsrc_ap()[0:1, TL:2 * TL],
                          -1.0, 2.0, ALU.mult, ALU.add)
        eng.tensor_scalar(lnsrc_ap()[0:1, TL:2 * TL], lnsrc_ap()[0:1, TL:2 * TL],
                          tab.ap()[0:1, 5:6], 1.0, ALU.mult, ALU.add)
        return eng.tensor_scalar_mul(lnsrc_ap()[0:1, 0:TL], lnsrc_ap()[0:1, 0:TL],
                                     tab.ap()[0:1, 1:2])
    P.op("vector", f_wp1, reads=[lnsrc_b, tab_b], writes=[lnsrc_b])

    def f_wp2(eng):
        return eng.activation(out=lnsrc_ap()[0:1, 0:TL], in_=lnsrc_ap()[0:1, 0:TL], func=AF.Exp)
    P.op("scalar", f_wp2, reads=[lnsrc_b], writes=[lnsrc_b])

    def f_wp3(eng):
        eng.tensor_tensor(lnsrc_ap()[0:1, 0:TL], lnsrc_ap()[0:1, 0:TL],
                          lnsrc_ap()[0:1, TL:2 * TL], ALU.mult)
        return eng.tensor_scalar_mul(lnsrc_ap()[0:1, 0:TL], lnsrc_ap()[0:1, 0:TL],
                                     gam.ap()[0:1, 1:2])
    P.op("vector", f_wp3, reads=[lnsrc_b, gam_b], writes=[lnsrc_b])

    for u in range(16):
        def f_wt(eng, u=u):
            return eng.matmul(psA.ap()[:, 16 + u:17 + u], lnsrc_ap()[0:1, u * 128:(u + 1) * 128],
                              tab.ap()[0:1, 4:5], start=True, stop=True)
        P.op("tensor", f_wt, reads=[lnsrc_b, tab_b], writes=[psA_b])
    def f_wtc(eng):
        return eng.copy(wprec.ap(), psA.ap()[:, 16:32])
    P.op("scalar", f_wtc, reads=[psA_b], writes=[wprec_b])

    for half in range(2):
        def f_bvb(eng, half=half):
            return eng.matmul(psB.ap(), ones_row.ap(), bvr.ap()[0:1, half * 512:(half + 1) * 512],
                              start=True, stop=True)
        P.op("tensor", f_bvb, reads=[bvr_b], writes=[psB_b])

        def f_bvc(eng, half=half):
            return eng.tensor_copy(bv_bc.ap()[:, half * 512:(half + 1) * 512], psB.ap())
        P.op("vector", f_bvc, reads=[psB_b], writes=[bv_bc_b])

    P.op("vector", lambda e: e.memset(S.ap(), 0.0), writes=[S_b])
    P.op("vector", lambda e: e.memset(z.ap()[:, 0:NC], 0.0), writes=[z_b])

    for sub in range(4):
        for u4 in range(4):
            u = sub * 4 + u4
            goff = (u % 4) * 128
            if u % 4 == 0:
                def f_ldxp(e, u=u):
                    gtg = u // 4
                    return e.dma_start(out=lnsrc_ap().rearrange("p (c t) -> p c t", c=NC),
                                       in_=xp_dram.ap().rearrange("(c p) t -> p c t", p=128)[:, :, gtg * 512:(gtg + 1) * 512])
                P.op("sync", f_ldxp, writes=[lnsrc_b], dma=True)
                emit_ln_tg(ctx, grp, lnsrc_b, NC, 0, xlng, xlng_b, NC, 0, src_bcf32=True)

            psk, psk_b = ctx.ps[0]
            psk2, psk2_b = ctx.ps[1]

            def f_kmm(eng, goff=goff):
                last = None
                for do in range(NC):
                    bank = psk if do < 4 else psk2
                    col = (do % 4) * 128
                    for dc in range(NC):
                        lhsT = wk.ap()[:, dc * D + do * 128: dc * D + do * 128 + 128]
                        rhs = xlng.ap().rearrange("p (c t) -> p c t", c=NC)[:, dc, goff:goff + 128]
                        last = eng.matmul(bank.ap()[:, col:col + 128], lhsT, rhs,
                                          start=(dc == 0), stop=(dc == NC - 1))
                return last
            P.op("tensor", f_kmm, reads=[xlng_b, wk_b], writes=[psk_b, psk2_b])

            def f_kev(eng):
                last = None
                for do in range(NC):
                    bank = psk if do < 4 else psk2
                    col = (do % 4) * 128
                    last = eng.activation(out=kfm_ap(do), in_=bank.ap()[:, col:col + 128],
                                          func=AF.Relu, bias=bk.ap()[:, do:do + 1])
                return last
            P.op("scalar", f_kev, reads=[psk_b, psk2_b, bk_b], writes=[kfm_b])

            pst, pst_b = ctx.ps[2]

            def f_ktr(eng):
                last = None
                for dc in range(NC):
                    last = eng.transpose(pst.ap().bitcast(BF16)[:, dc * 128:dc * 128 + 128],
                                         kfm_ap(dc), ident.ap())
                return last
            P.op("tensor", f_ktr, reads=[kfm_b, ident_b], writes=[pst_b])

            def f_ktv(eng, u=u, u4=u4):
                last = None
                for dc in range(NC):
                    o = pre.ap()[:, u4 * 1024 + dc * 128: u4 * 1024 + dc * 128 + 128]
                    last = eng.tensor_scalar_mul(o, pst.ap().bitcast(BF16)[:, dc * 128:dc * 128 + 128],
                                                 wprec.ap()[:, u:u + 1])
                return last
            P.op("vector", f_ktv, reads=[pst_b, wprec_b], writes=[pre_b])

            for eh in range(2):
                psv, psv_b = ctx.ps[3 + eh]

                def f_vmm(eng, goff=goff, eh=eh, psv=psv):
                    last = None
                    for dc in range(NC):
                        lhsT = xlng.ap().rearrange("p (c t) -> p c t", c=NC)[:, dc, goff:goff + 128]
                        rhs = wv.ap()[:, dc * D + eh * 512: dc * D + eh * 512 + 512]
                        last = eng.matmul(psv.ap(), lhsT, rhs, start=(dc == 0), stop=(dc == NC - 1))
                    return last
                P.op("tensor", f_vmm, reads=[xlng_b, wv_b], writes=[psv_b])

                def f_vev(eng, u4=u4, eh=eh, psv=psv):
                    o = pre.ap()[:, 4096 + u4 * 1024 + eh * 512: 4096 + u4 * 1024 + eh * 512 + 512]
                    with ALP(reason="bf16 v evac"):
                        return eng.tensor_tensor(o, psv.ap(), bv_bc.ap()[:, eh * 512:(eh + 1) * 512], ALU.add)
                P.op("vector", f_vev, reads=[psv_b, bv_bc_b], writes=[pre_b])

        for dt in range(NC):
            for eh in range(2):
                pss, pss_b = ctx.ps[eh]

                def f_smm(eng, dt=dt, eh=eh, pss=pss):
                    last = None
                    for u4 in range(4):
                        lhsT = pre.ap()[:, u4 * 1024 + dt * 128: u4 * 1024 + dt * 128 + 128]
                        rhs = pre.ap()[:, 4096 + u4 * 1024 + eh * 512: 4096 + u4 * 1024 + eh * 512 + 512]
                        last = eng.matmul(pss.ap(), lhsT, rhs, start=(u4 == 0), stop=(u4 == 3))
                    return last
                P.op("tensor", f_smm, reads=[pre_b], writes=[pss_b])

                def f_sadd(eng, dt=dt, eh=eh, pss=pss):
                    o = S.ap().rearrange("p (c t) -> p c t", c=NC)[:, dt, eh * 512:(eh + 1) * 512]
                    with ALP(reason="bf16 state"):
                        return eng.tensor_add(o, o, pss.ap())
                P.op("vector", f_sadd, reads=[pss_b, S_b], writes=[S_b])

        psz, psz_b = ctx.ps[2]

        def f_zmmp(eng):
            last = None
            for dt in range(NC):
                for u4 in range(4):
                    lhsT = pre.ap()[:, u4 * 1024 + dt * 128: u4 * 1024 + dt * 128 + 128]
                    last = eng.matmul(psz.ap()[:, dt:dt + 1], lhsT, ones_colb.ap(),
                                      start=(u4 == 0), stop=(u4 == 3))
            return last
        P.op("tensor", f_zmmp, reads=[pre_b], writes=[psz_b])

        def f_zaddp(eng):
            return eng.tensor_add(z.ap()[:, 0:NC], z.ap()[:, 0:NC], psz.ap()[:, 0:NC])
        P.op("vector", f_zaddp, reads=[psz_b, z_b], writes=[z_b])

    def f_ldwq(e):
        return e.dma_start(out=pre.ap().rearrange("p (c d) -> p c d", c=NC),
                           in_=ctx.dram["w_q"].ap().transpose([1, 0, 2]))
    P.op("sync", f_ldwq, writes=[pre_b], dma=True)
    wq, wq_b = pre, pre_b

    grp_b = Buf("a0_grp")
    for g in range(A0_NGRP):
        t0 = g * 512
        emit_ln_tg(ctx, xt, xt_b, NC, t0, xlng, xlng_b, NC, 0)

        for wi, (wob, wbb, bb, bbuf, off) in enumerate(
                ((wq, wq_b, bq, bq_b, 0), (wk, wk_b, bk, bk_b, 4096))):
            for do in range(NC):
                psp, psp_b = ctx.ps[do % 2]

                def f_mm(eng, wob=wob, do=do, psp=psp):
                    last = None
                    for dc in range(NC):
                        lhsT = wob.ap()[:, dc * D + do * 128: dc * D + do * 128 + 128]
                        rhs = xlng.ap().rearrange("p (c t) -> p c t", c=NC)[:, dc, :]
                        last = eng.matmul(psp.ap(), lhsT, rhs, start=(dc == 0), stop=(dc == NC - 1))
                    return last
                P.op("tensor", f_mm, reads=[xlng_b, wbb], writes=[psp_b])

                def f_ev(eng, bb=bb, do=do, off=off, psp=psp):
                    o = grp.ap()[:, off + do * 512: off + do * 512 + 512]
                    return eng.activation(out=o, in_=psp.ap(), func=AF.Relu,
                                          bias=bb.ap()[:, do:do + 1])
                P.op("scalar", f_ev, reads=[psp_b, bbuf], writes=[grp_b])

        for ci in range(4):
            for eh in range(2):
                psv, psv_b = ctx.ps[2 + (ci * 2 + eh) % 2]

                def f_vmm(eng, ci=ci, eh=eh, psv=psv):
                    last = None
                    for dc in range(NC):
                        lhsT = xlng.ap().rearrange("p (c t) -> p c t", c=NC)[:, dc, ci * 128:ci * 128 + 128]
                        rhs = wv.ap()[:, dc * D + eh * 512: dc * D + eh * 512 + 512]
                        last = eng.matmul(psv.ap(), lhsT, rhs, start=(dc == 0), stop=(dc == NC - 1))
                    return last
                P.op("tensor", f_vmm, reads=[xlng_b, wv_b], writes=[psv_b])

                def f_vev(eng, ci=ci, eh=eh, psv=psv):
                    o = grp.ap()[:, 8192 + ci * 1024 + eh * 512: 8192 + ci * 1024 + eh * 512 + 512]
                    with ALP(reason="bf16 v evac"):
                        return eng.tensor_tensor(o, psv.ap(), bv_bc.ap()[:, eh * 512:(eh + 1) * 512], ALU.add)
                P.op("vector", f_vev, reads=[psv_b, bv_bc_b], writes=[grp_b])

        for ci in range(A0_NCHUNK):
            tloc = t0 + ci * 128

            def qt_ap(dc, ci=ci):
                return grp.ap()[:, dc * 512 + ci * 128: dc * 512 + ci * 128 + 128]

            def kt_ap(dc, ci=ci):
                return grp.ap()[:, 4096 + dc * 512 + ci * 128: 4096 + dc * 512 + ci * 128 + 128]

            def vtm_ap(et, ci=ci):
                return grp.ap()[:, 8192 + ci * 1024 + et * 128: 8192 + ci * 1024 + et * 128 + 128]

            psat, psat_b = ctx.ps[0]

            def f_at(eng, kt_ap=kt_ap, qt_ap=qt_ap):
                last = None
                for dc in range(NC):
                    last = eng.matmul(psat.ap()[:, 0:128], kt_ap(dc), qt_ap(dc),
                                      start=(dc == 0), stop=(dc == NC - 1))
                return last
            P.op("tensor", f_at, reads=[grp_b], writes=[psat_b])

            ab, ab_b = ctx.get("lnx2")

            def f_ab(eng):
                with ALP(reason="bf16 scores"):
                    return eng.tensor_tensor(ab.ap()[:, 0:128], psat.ap()[:, 0:128],
                                             mdec.ap(), ALU.mult)
            P.op("vector", f_ab, reads=[psat_b, mdec_b], writes=[ab_b])

            qd, qd_b = ctx.get("lnxb")

            def f_qt(eng, qt_ap=qt_ap):
                last = None
                for dc in range(NC):
                    last = eng.tensor_tensor(qd.ap()[:, dc * 128:dc * 128 + 128],
                                             qt_ap(dc), qdecb.ap(), ALU.mult)
                return last
            P.op("vector", f_qt, reads=[grp_b, qdecb_b], writes=[qd_b])

            def f_zbf(eng):
                with ALP(reason="bf16 z"):
                    return eng.tensor_copy(zbf.ap(), z.ap()[:, 0:NC])
            P.op("vector", f_zbf, reads=[z_b], writes=[zbf_b])

            psd, psd_b = ctx.ps[3]

            def f_den(eng):
                if A0_SKIP_INTER:
                    return eng.matmul(psd.ap()[0:1, 0:128], ones_colb.ap(), ab.ap()[:, 0:128],
                                      start=True, stop=True)
                eng.matmul(psd.ap()[0:1, 0:128], ones_colb.ap(), ab.ap()[:, 0:128],
                           start=True, stop=False)
                last = None
                for dc in range(NC):
                    last = eng.matmul(psd.ap()[0:1, 0:128], zbf.ap()[:, dc:dc + 1],
                                      qd.ap()[:, dc * 128:dc * 128 + 128],
                                      start=False, stop=(dc == NC - 1))
                return last
            P.op("tensor", f_den, reads=[ab_b, qd_b, zbf_b], writes=[psd_b])

            def f_rec(eng):
                eng.tensor_scalar_add(tmp.ap()[0:1, 0:128], psd.ap()[0:1, 0:128], EPS)
                return eng.reciprocal(tmp.ap()[0:1, 128:256], tmp.ap()[0:1, 0:128])
            P.op("vector", f_rec, reads=[psd_b], writes=[tmp_b])

            psr, psr_b = ctx.ps[4]

            def f_rbc(eng):
                return eng.matmul(psr.ap()[:, 0:128], ones_row.ap(), tmp.ap()[0:1, 128:256],
                                  start=True, stop=True)
            P.op("tensor", f_rbc, reads=[tmp_b], writes=[psr_b])

            def f_abr(eng):
                with ALP(reason="bf16 scores"):
                    return eng.tensor_tensor(ab.ap()[:, 0:128], ab.ap()[:, 0:128],
                                             psr.ap()[:, 0:128], ALU.mult)
            P.op("vector", f_abr, reads=[ab_b, psr_b], writes=[ab_b])

            def f_qdr(eng):
                last = None
                for dc in range(NC):
                    with ALP(reason="bf16 q"):
                        last = eng.tensor_tensor(qd.ap()[:, dc * 128:dc * 128 + 128],
                                                 qd.ap()[:, dc * 128:dc * 128 + 128],
                                                 psr.ap()[:, 0:128], ALU.mult)
                return last
            P.op("vector", f_qdr, reads=[qd_b, psr_b], writes=[qd_b])

            for ebank in range(2):
                psn, psn_b = ctx.ps[(1, 2)[ebank]]

                def f_num(eng, ebank=ebank, psn=psn, vtm_ap=vtm_ap):
                    last = None
                    for et4 in range(4):
                        et = ebank * 4 + et4
                        o = psn.ap()[:, et4 * 128:et4 * 128 + 128]
                        if A0_SKIP_INTER:
                            last = eng.matmul(o, vtm_ap(et), ab.ap()[:, 0:128], start=True, stop=True)
                            continue
                        eng.matmul(o, vtm_ap(et), ab.ap()[:, 0:128], start=True, stop=False)
                        for dc in range(NC):
                            lhsT = S.ap().rearrange("p (c t) -> p c t", c=NC)[:, dc, et * 128:et * 128 + 128]
                            last = eng.matmul(o, lhsT, qd.ap()[:, dc * 128:dc * 128 + 128],
                                              start=False, stop=(dc == NC - 1))
                    return last
                P.op("tensor", f_num, reads=[grp_b, ab_b, qd_b, S_b], writes=[psn_b])

                def f_oadd(eng, ebank=ebank, psn=psn, tloc=tloc):
                    last = None
                    for et4 in range(4):
                        et = ebank * 4 + et4
                        o = xt.ap().rearrange("p (c t) -> p c t", c=NC)[:, et, tloc:tloc + 128]
                        last = eng.tensor_add(o, o, psn.ap()[:, et4 * 128:et4 * 128 + 128])
                    return last
                P.op("vector", f_oadd, reads=[psn_b, xt_b], writes=[xt_b])

            pst, pst_b = ctx.ps[5]

            def f_ktr(eng, kt_ap=kt_ap):
                last = None
                for dc in range(NC):
                    last = eng.transpose(pst.ap().bitcast(BF16)[:, dc * 128:dc * 128 + 128],
                                         kt_ap(dc), ident.ap())
                return last
            P.op("tensor", f_ktr, reads=[grp_b, ident_b], writes=[pst_b])

            def f_ktv(eng):
                last = None
                for dc in range(NC):
                    last = eng.tensor_scalar_mul(ktm.ap()[:, dc * 128:dc * 128 + 128],
                                                 pst.ap().bitcast(BF16)[:, dc * 128:dc * 128 + 128],
                                                 kdec.ap()[:, 0:1])
                return last
            P.op("vector", f_ktv, reads=[pst_b, kdec_b], writes=[ktm_b])

            psz, psz_b = ctx.ps[6]

            def f_zmm(eng):
                last = None
                for dt in range(NC):
                    last = eng.matmul(psz.ap()[:, dt:dt + 1], ktm.ap()[:, dt * 128:dt * 128 + 128],
                                      ones_colb.ap(), start=True, stop=True)
                return last
            P.op("tensor", f_zmm, reads=[ktm_b], writes=[psz_b])

            def f_zup(eng):
                return eng.scalar_tensor_tensor(z.ap()[:, 0:NC], z.ap()[:, 0:NC],
                                                kdec.ap()[:, 1:2], psz.ap()[:, 0:NC],
                                                ALU.mult, ALU.add)
            P.op("vector", f_zup, reads=[z_b, kdec_b, psz_b], writes=[z_b])

            for dt in range(NC):
                for eh in range(2):
                    pss, pss_b = ctx.ps[(5, 7)[eh]]

                    def f_dsmm(eng, dt=dt, eh=eh, pss=pss, ci=ci):
                        return eng.matmul(pss.ap(), ktm.ap()[:, dt * 128:dt * 128 + 128],
                                          grp.ap()[:, 8192 + ci * 1024 + eh * 512: 8192 + ci * 1024 + eh * 512 + 512],
                                          start=True, stop=True)
                    P.op("tensor", f_dsmm, reads=[ktm_b, grp_b], writes=[pss_b])

                    def f_sup(eng, dt=dt, eh=eh, pss=pss):
                        o = S.ap().rearrange("p (c t) -> p c t", c=NC)[:, dt, eh * 512:(eh + 1) * 512]
                        with ALP(reason="bf16 state"):
                            return eng.scalar_tensor_tensor(o, o, kdec.ap()[:, 1:2], pss.ap(),
                                                            ALU.mult, ALU.add)
                    P.op("vector", f_sup, reads=[S_b, kdec_b, pss_b], writes=[S_b])



def emit_mha(ctx, x_dram, xp_dram, *, head_ln=True, q_scale=None, mask_mode="a1",
             has_bo=True):
    nc, P = ctx.nc, ctx.P
    stg, stg_b = ctx.get("lnsrc")
    xln, xln_b = ctx.get("a1_xln")
    xlnp, xlnp_b = ctx.get("a1_xlnp")
    QT, QT_b = ctx.get("a1_qt")
    KT, KT_b = ctx.get("a1_kt")
    vf, vf_b = ctx.get("a1_vf")
    VT, VT_b = ctx.get("a1_vtm")
    kh, kh_b = ctx.get("a1_khat")
    qh, qh_b = ctx.get("a1_qhat")
    Pb, Pb_b = ctx.get("a1_p")
    OT, OT_b = ctx.get("a1_ot")
    mst, mst_b = ctx.get("a1_m")
    if mask_mode == "b0":
        wsl, wsl_b = ctx.get("b0_m")
    else:
        wsl, wsl_b = ctx.get("a1_w")
    gqb, gqb_b = ctx.get("a1_gqb")
    bo, bo_b = ctx.get("v_a1_bo")
    e64, e64_b = ctx.get("a1_e64")
    E2, E2_b = ctx.get("a1_E2")
    if mask_mode == "a1":
        utri, utri_b = ctx.get("a1_utri")
        ut512, ut512_b = ctx.get("a1_ut512")
    else:
        utri = utri_b = ut512 = ut512_b = None
        b0m, b0m_b = wsl, wsl_b
    ident, ident_b = ctx.get("ident")
    pfl, pfl_b = ctx.get("a1_pfl")
    ceps, ceps_b = ctx.get("ceps")
    x2s, x2s_b = ctx.get("lnx2")
    xbs, xbs_b = ctx.get("lnxb")
    tmp, tmp_b = ctx.get("lntmp")
    ost, ost_b = ctx.get("a1_ost")
    onescb, _ = ctx.get("ones_colb")
    ones_row, _ = ctx.get("ones_row")
    bigps, bigps_b = ctx.big_ps
    ALP = ctx.nc.allow_low_precision

    for (src_dram, dsth, dstb) in ((x_dram, xln, xln_b), (xp_dram, xlnp, xlnp_b)):
        for tg in range(4):
            def f_ld(e, src_dram=src_dram, tg=tg):
                return e.dma_start(out=stg.ap().rearrange("p (c t) -> p c t", c=NC),
                                   in_=src_dram.ap().rearrange("(c p) t -> p c t", p=128)[:, :, tg * 512:(tg + 1) * 512])
            P.op("sync", f_ld, writes=[stg_b], dma=True)
            emit_ln_tg(ctx, stg, stg_b, NC, 0, dsth, dstb, NC, tg * 512)

    P.op("vector", lambda e: e.memset(kh.ap()[64:65, :], 1.0), writes=[kh_b])

    def head_ln_fn(buf, bufb, n_tg, gcol, bcol):
        for tg in range(n_tg):
            psA, psA_b = ctx.ps[2]
            psB, psB_b = ctx.ps[3]

            def f_sq(eng, buf=buf, tg=tg):
                return eng.activation(out=x2s.ap()[:, 0:512],
                                      in_=buf.ap()[:, tg * 512:(tg + 1) * 512], func=AF.Square)
            P.op("scalar", f_sq, reads=[bufb], writes=[x2s_b])

            def f_sums(eng, buf=buf, tg=tg, psA=psA, psB=psB):
                eng.matmul(psA.ap()[0:2, 0:512], e64.ap(),
                           buf.ap()[:, tg * 512:(tg + 1) * 512], start=True, stop=True)
                return eng.matmul(psB.ap()[0:2, 0:512], e64.ap(), x2s.ap()[:, 0:512],
                                  start=True, stop=True)
            P.op("tensor", f_sums, reads=[bufb, x2s_b, e64_b], writes=[psA_b, psB_b])

            def f_stats(eng, psA=psA, psB=psB):
                eng.tensor_scalar_mul(tmp.ap()[0:2, 0:512], psA.ap()[0:2, 0:512], 1.0 / 64)
                eng.tensor_scalar_mul(x2s.ap().bitcast(F32)[0:2, 512:1024],
                                      psB.ap()[0:2, 0:512], 1.0 / 64)
                eng.tensor_tensor(xbs.ap().bitcast(F32)[0:2, 0:512], tmp.ap()[0:2, 0:512],
                                  tmp.ap()[0:2, 0:512], ALU.mult)
                return eng.tensor_sub(x2s.ap().bitcast(F32)[0:2, 512:1024],
                                      x2s.ap().bitcast(F32)[0:2, 512:1024],
                                      xbs.ap().bitcast(F32)[0:2, 0:512])
            P.op("vector", f_stats, reads=[psA_b, psB_b], writes=[tmp_b, x2s_b, xbs_b])

            def f_rstd(eng):
                eng.activation(out=x2s.ap().bitcast(F32)[0:2, 512:1024],
                               in_=x2s.ap().bitcast(F32)[0:2, 512:1024], func=AF.Ln,
                               bias=ceps.ap()[0:2, 0:1])
                return eng.activation(out=x2s.ap().bitcast(F32)[0:2, 512:1024],
                                      in_=x2s.ap().bitcast(F32)[0:2, 512:1024],
                                      func=AF.Exp, scale=-0.5)
            P.op("scalar", f_rstd, reads=[x2s_b, ceps_b], writes=[x2s_b])

            def f_bb(eng):
                return eng.scalar_tensor_tensor(xbs.ap().bitcast(F32)[0:2, 512:1024],
                                                tmp.ap()[0:2, 0:512],
                                                -1.0, x2s.ap().bitcast(F32)[0:2, 512:1024],
                                                ALU.mult, ALU.mult)
            P.op("vector", f_bb, reads=[tmp_b, x2s_b], writes=[xbs_b])

            def f_cast(eng):
                with ALP(reason="bf16 ln rows"):
                    eng.tensor_copy(x2s.ap()[0:2, 2048:2560], x2s.ap().bitcast(F32)[0:2, 512:1024])
                    return eng.tensor_copy(x2s.ap()[0:2, 2560:3072], xbs.ap().bitcast(F32)[0:2, 512:1024])
            P.op("vector", f_cast, reads=[x2s_b, xbs_b], writes=[x2s_b])

            def f_bc(eng, psA=psA, psB=psB):
                eng.matmul(psA.ap(), E2.ap(), x2s.ap()[0:2, 2048:2560],
                           start=True, stop=True)
                return eng.matmul(psB.ap(), E2.ap(), x2s.ap()[0:2, 2560:3072],
                                  start=True, stop=True)
            P.op("tensor", f_bc, reads=[x2s_b, E2_b], writes=[psA_b, psB_b])

            def f_apply(eng, buf=buf, tg=tg, gcol=gcol, bcol=bcol, psA=psA, psB=psB):
                sl = buf.ap()[:, tg * 512:(tg + 1) * 512]
                with ALP(reason="bf16 headln"):
                    eng.tensor_tensor(x2s.ap()[:, 512:1024], sl, psA.ap(), ALU.mult)
                    eng.tensor_tensor(x2s.ap()[:, 512:1024], x2s.ap()[:, 512:1024],
                                      psB.ap(), ALU.add)
                    return eng.tensor_scalar(sl, x2s.ap()[:, 512:1024],
                                             gqb.ap()[:, gcol:gcol + 1],
                                             gqb.ap()[:, bcol:bcol + 1], ALU.mult, ALU.add)
            P.op("vector", f_apply, reads=[ctx.ps[2][1], ctx.ps[3][1], gqb_b, bufb, x2s_b],
                 writes=[bufb, x2s_b])

    for hp in range(NC):
        for wi, dname in enumerate(("w_q", "w_k", "w_v")):
            def f_ldw(e, wi=wi, dname=dname, hp=hp):
                src = ctx.dram[dname].ap()[:, :, hp * 128:(hp + 1) * 128]
                return e.dma_start(
                    out=wsl.ap()[:, wi * NC * 128:(wi + 1) * NC * 128].rearrange("p (c d) -> p c d", c=NC),
                    in_=src.transpose([1, 0, 2]))
            P.op("sync", f_ldw, writes=[wsl_b], dma=True)

        def wslice(wi, dc):
            return wsl.ap()[:, wi * NC * 128 + dc * 128: wi * NC * 128 + dc * 128 + 128]

        for tg in range(4):
            psp, psp_b = ctx.ps[tg % 2]

            def f_qmm(eng, tg=tg, psp=psp, wslice=wslice):
                last = None
                for dc in range(NC):
                    last = eng.matmul(psp.ap(), wslice(0, dc),
                                      xln.ap().rearrange("p (c t) -> p c t", c=NC)[:, dc, tg * 512:(tg + 1) * 512],
                                      start=(dc == 0), stop=(dc == NC - 1))
                return last
            P.op("tensor", f_qmm, reads=[xln_b, wsl_b], writes=[psp_b])

            if q_scale is None:
                def f_qev(eng, tg=tg, psp=psp):
                    return eng.tensor_copy(QT.ap()[:, tg * 512:(tg + 1) * 512], psp.ap())
                P.op("vector", f_qev, reads=[psp_b], writes=[QT_b])
            else:
                def f_qev(eng, tg=tg, psp=psp):
                    with ctx.nc.allow_low_precision(reason="bf16 q"):
                        return eng.tensor_scalar_mul(QT.ap()[:, tg * 512:(tg + 1) * 512],
                                                     psp.ap(), q_scale)
                P.op("vector", f_qev, reads=[psp_b], writes=[QT_b])

        for half, (srch, srcb) in enumerate(((xlnp, xlnp_b), (xln, xln_b))):
            for tg in range(4):
                psp, psp_b = ctx.ps[tg % 2]

                def f_kmm(eng, srch=srch, tg=tg, psp=psp, wslice=wslice):
                    last = None
                    for dc in range(NC):
                        last = eng.matmul(psp.ap(), wslice(1, dc),
                                          srch.ap().rearrange("p (c t) -> p c t", c=NC)[:, dc, tg * 512:(tg + 1) * 512],
                                          start=(dc == 0), stop=(dc == NC - 1))
                    return last
                P.op("tensor", f_kmm, reads=[srcb, wsl_b], writes=[psp_b])

                def f_kev(eng, half=half, tg=tg, psp=psp):
                    return eng.tensor_copy(KT.ap()[:, half * TL + tg * 512: half * TL + tg * 512 + 512],
                                           psp.ap())
                P.op("vector", f_kev, reads=[psp_b], writes=[KT_b])

        if head_ln:
            head_ln_fn(QT, QT_b, 4, 0, 1)
            head_ln_fn(KT, KT_b, 8, 2, 3)

        for half, (srch, srcb) in enumerate(((xlnp, xlnp_b), (xln, xln_b))):
            for tg in range(4):
                psp, psp_b = ctx.ps[2]

                def f_vmm(eng, srch=srch, tg=tg, psp=psp, wslice=wslice):
                    last = None
                    for dc in range(NC):
                        last = eng.matmul(psp.ap(), wslice(2, dc),
                                          srch.ap().rearrange("p (c t) -> p c t", c=NC)[:, dc, tg * 512:(tg + 1) * 512],
                                          start=(dc == 0), stop=(dc == NC - 1))
                    return last
                P.op("tensor", f_vmm, reads=[srcb, wsl_b], writes=[psp_b])

                def f_vev(eng, psp=psp):
                    return eng.tensor_copy(vf.ap(), psp.ap())
                P.op("vector", f_vev, reads=[psp_b], writes=[vf_b])

                pst, pst_b = ctx.ps[3]

                def f_vtr(eng, pst=pst):
                    last = None
                    for c4 in range(4):
                        last = eng.transpose(pst.ap().bitcast(BF16)[:, c4 * 128:c4 * 128 + 128],
                                             vf.ap()[:, c4 * 128:c4 * 128 + 128], ident.ap())
                    return last
                P.op("tensor", f_vtr, reads=[vf_b, ident_b], writes=[pst_b])

                def f_vte(eng, half=half, tg=tg, pst=pst):
                    base = half * 16 * 128 + tg * 4 * 128
                    return eng.tensor_copy(VT.ap()[:, base:base + 512],
                                           pst.ap().bitcast(BF16)[:, 0:512])
                P.op("vector", f_vte, reads=[pst_b], writes=[VT_b])

        for hd_i in range(2):
            head_rows = slice(hd_i * 64, hd_i * 64 + 64)

            def f_khd(e, head_rows=head_rows):
                return e.dma_start(out=kh.ap()[0:64, :], in_=KT.ap()[head_rows, :])
            P.op("gpsimd", f_khd, reads=[KT_b], writes=[kh_b], dma=True)

            def f_qhd(e, head_rows=head_rows):
                return e.dma_start(out=qh.ap()[0:64, :], in_=QT.ap()[head_rows, :])
            P.op("gpsimd", f_qhd, reads=[QT_b], writes=[qh_b], dma=True)

            if mask_mode == "b0":
                def f_ldmq(e):
                    return e.dma_start(out=b0m.ap()[:, 0:len(B0Q_KEYS) * 512],
                                       in_=ctx.dram["b0mq"].ap())
                P.op("sync", f_ldmq, writes=[b0m_b], dma=True)

            for qt in range(16):
                own_w = ((qt // 4) + 1) * 512

                def f_s1(eng, qt=qt, head_rows=head_rows, hd_i=hd_i, hp=hp):
                    last = None
                    for seg in range(4):
                        nm = (mask_mode == "b0")
                        last = eng.matmul(bigps.ap()[:, seg * 512:(seg + 1) * 512],
                                          QT.ap()[head_rows, qt * 128:qt * 128 + 128],
                                          KT.ap()[head_rows, seg * 512:(seg + 1) * 512],
                                          start=True, stop=not nm)
                        if nm:
                            v = ctx.b0_variant_q(hp, qt, seg, prefix=True)
                            last = eng.matmul(bigps.ap()[:, seg * 512:(seg + 1) * 512],
                                              ident.ap(), b0m.ap()[:, v * 512:(v + 1) * 512],
                                              start=False, stop=True)
                    return last
                P.op("tensor", f_s1,
                     reads=[QT_b, KT_b] + ([b0m_b, ident_b] if mask_mode == "b0" else []),
                     writes=[bigps_b])

                def f_m1(eng):
                    return eng.reduce_max(mst.ap()[:, 32:33], bigps.ap()[:, 0:2048], axis=AX.X)
                P.op("vector", f_m1, reads=[bigps_b], writes=[mst_b])

                def f_s2(eng, qt=qt, own_w=own_w, head_rows=head_rows, hp=hp):
                    nseg = own_w // 512
                    last = None
                    for seg in range(nseg):
                        if mask_mode == "b0":
                            last = eng.matmul(bigps.ap()[:, seg * 512:(seg + 1) * 512],
                                              QT.ap()[head_rows, qt * 128:qt * 128 + 128],
                                              KT.ap()[head_rows, TL + seg * 512:TL + seg * 512 + 512],
                                              start=True, stop=False)
                            v = ctx.b0_variant_q(hp, qt, seg, prefix=False)
                            last = eng.matmul(bigps.ap()[:, seg * 512:(seg + 1) * 512],
                                              ident.ap(), b0m.ap()[:, v * 512:(v + 1) * 512],
                                              start=False, stop=True)
                        else:
                            eng.matmul(bigps.ap()[:, seg * 512:(seg + 1) * 512],
                                       QT.ap()[head_rows, qt * 128:qt * 128 + 128],
                                       KT.ap()[head_rows, TL + seg * 512:TL + seg * 512 + 512],
                                       start=True, stop=(seg != nseg - 1))
                    if mask_mode == "b0":
                        return last
                    c0 = (qt % 4) * 128
                    return eng.matmul(bigps.ap()[:, (nseg - 1) * 512 + c0:(nseg - 1) * 512 + c0 + 128],
                                      utri.ap(), ident.ap(), start=False, stop=True)
                P.op("tensor", f_s2,
                     reads=([QT_b, KT_b, ident_b] + ([b0m_b] if mask_mode == "b0" else [utri_b])),
                     writes=[bigps_b])

                def f_m2(eng, qt=qt, own_w=own_w):
                    eng.reduce_max(mst.ap()[:, 33:34], bigps.ap()[:, 0:own_w], axis=AX.X)
                    return eng.tensor_max(mst.ap()[:, qt:qt + 1], mst.ap()[:, 32:33],
                                          mst.ap()[:, 33:34])
                P.op("vector", f_m2, reads=[bigps_b, mst_b], writes=[mst_b])

            def f_negm(eng):
                with ALP(reason="bf16 shift"):
                    return eng.tensor_scalar_mul(mst.ap().bitcast(BF16)[:, 96:112],
                                                 mst.ap()[:, 0:16], -1.0)
            P.op("vector", f_negm, reads=[mst_b], writes=[mst_b])

            for qt in range(16):
                psn, psn_b = ctx.ps[3]

                def f_ntr(eng, qt=qt, psn=psn):
                    return eng.transpose(psn.ap().bitcast(BF16)[0:1, 0:128],
                                         mst.ap().bitcast(BF16)[:, 96 + qt:97 + qt], ident.ap())
                P.op("tensor", f_ntr, reads=[mst_b, ident_b], writes=[psn_b])

                def f_ncp(eng, qt=qt, psn=psn):
                    return eng.tensor_copy(qh.ap()[64:65, qt * 128:qt * 128 + 128],
                                           psn.ap().bitcast(BF16)[0:1, 0:128])
                P.op("vector", f_ncp, reads=[psn_b], writes=[qh_b])

            if mask_mode == "b0":
                def f_ldmt(e):
                    return e.dma_start(out=b0m.ap()[:, 0:len(B0T_KEYS) * 512],
                                       in_=ctx.dram["b0mt"].ap())
                P.op("sync", f_ldmt, writes=[b0m_b], dma=True)

            for qtb in range(4):
                own_chunks = (qtb + 1) * 4
                kcs = list(range(16)) + [16 + c for c in range(own_chunks)]
                psA, psA_b = ctx.ps[0]
                psD, psD_b = ctx.ps[1]
                psDp, psDp_b = ctx.ps[3]
                groups = [kcs[g0:g0 + 4] for g0 in range(0, len(kcs), 4)]
                for gidx, grp_k in enumerate(groups):
                    lastg = gidx == len(groups) - 1

                    def f_st2(eng, grp_k=tuple(grp_k), qtb=qtb, hp=hp):
                        last = None
                        for gi, kc in enumerate(grp_k):
                            if mask_mode == "b0":
                                last = eng.matmul(bigps.ap()[:, gi * 512:(gi + 1) * 512],
                                                  kh.ap()[0:65, kc * 128:kc * 128 + 128],
                                                  qh.ap()[0:65, qtb * 512:(qtb + 1) * 512],
                                                  start=True, stop=False)
                                v = ctx.b0_variant_t(hp, kc, qtb)
                                last = eng.matmul(bigps.ap()[:, gi * 512:(gi + 1) * 512],
                                                  ident.ap(), b0m.ap()[:, v * 512:(v + 1) * 512],
                                                  start=False, stop=True)
                                continue
                            need_mask = kc >= 16 and (kc - 16) // 4 == qtb
                            last = eng.matmul(bigps.ap()[:, gi * 512:(gi + 1) * 512],
                                              kh.ap()[0:65, kc * 128:kc * 128 + 128],
                                              qh.ap()[0:65, qtb * 512:(qtb + 1) * 512],
                                              start=True, stop=not need_mask)
                            if need_mask:
                                j0 = (kc - 16) % 4
                                last = eng.matmul(bigps.ap()[:, gi * 512:(gi + 1) * 512],
                                                  ident.ap(), ut512.ap()[:, j0 * 512:(j0 + 1) * 512],
                                                  start=False, stop=True)
                        return last
                    P.op("tensor", f_st2,
                         reads=([kh_b, qh_b, ident_b] + ([b0m_b] if mask_mode == "b0" else [ut512_b])),
                         writes=[bigps_b])

                    def f_exp(eng, n=len(grp_k)):
                        return eng.activation(out=Pb.ap()[:, 0:n * 512],
                                              in_=bigps.ap()[:, 0:n * 512], func=AF.Exp)
                    P.op("scalar", f_exp, reads=[bigps_b], writes=[Pb_b])

                    def f_av(eng, grp_k=tuple(grp_k), gidx=gidx, lastg=lastg, hd_i=hd_i, psA=psA):
                        last = None
                        for gi, kc in enumerate(grp_k):
                            lhsT = VT.ap()[:, kc * 128 + hd_i * 64: kc * 128 + hd_i * 64 + 64]
                            kwargs = {}
                            if hd_i == 0:
                                oap = psA.ap()[0:64, :]
                            else:
                                oap = psA.ap()[64:128, :]
                                kwargs["tile_position"] = (0, 64)
                            last = eng.matmul(oap, lhsT, Pb.ap()[:, gi * 512:(gi + 1) * 512],
                                              start=(gidx == 0 and gi == 0),
                                              stop=(lastg and gi == len(grp_k) - 1), **kwargs)
                        return last
                    P.op("tensor", f_av, reads=[VT_b, Pb_b], writes=[psA_b])

                    def f_den(eng, grp_k=tuple(grp_k), lastg=lastg, psD=psD, psDp=psDp):
                        last = None
                        for gi, kc in enumerate(grp_k):
                            region = psD.ap()[0:1, 0:512] if kc >= 16 else psDp.ap()[0:1, 0:512]
                            last = eng.matmul(region, onescb.ap(), Pb.ap()[:, gi * 512:(gi + 1) * 512],
                                              start=(kc == 16 or kc == 0),
                                              stop=(lastg and gi == len(grp_k) - 1))
                        return last
                    P.op("tensor", f_den, reads=[Pb_b], writes=[psD_b, psDp_b])

                def f_dcomb(eng, psD=psD, psDp=psDp):
                    eng.tensor_copy(ost.ap()[0:1, 0:512], psDp.ap()[0:1, 0:512])
                    eng.scalar_tensor_tensor(tmp.ap()[0:1, 0:512], ost.ap()[0:1, 0:512],
                                             pfl.ap()[0:1, 0:1], psD.ap()[0:1, 0:512],
                                             ALU.mult, ALU.add)
                    return eng.reciprocal(ost.ap()[0:1, 512:1024], tmp.ap()[0:1, 0:512])
                P.op("vector", f_dcomb, reads=[psD_b, psDp_b, pfl_b], writes=[tmp_b, ost_b])

                psR, psR_b = ctx.ps[2]

                def f_rbc(eng, hd_i=hd_i, psR=psR):
                    if hd_i == 0:
                        return eng.matmul(psR.ap()[0:64, 0:512], ones_row.ap()[0:1, 0:64],
                                          ost.ap()[0:1, 512:1024], start=True, stop=True)
                    return eng.matmul(psR.ap()[64:128, 0:512], ones_row.ap()[0:1, 0:64],
                                      ost.ap()[0:1, 512:1024], start=True, stop=True,
                                      tile_position=(0, 64))
                P.op("tensor", f_rbc, reads=[ost_b], writes=[psR_b])

                def f_rbe(eng, hd_i=hd_i, psR=psR):
                    rows = slice(hd_i * 64, hd_i * 64 + 64)
                    return eng.tensor_copy(ost.ap()[rows, 0:512], psR.ap()[rows, 0:512])
                P.op("vector", f_rbe, reads=[psR_b], writes=[ost_b])

                def f_div(eng, hd_i=hd_i, hp=hp, qtb=qtb, psA=psA):
                    rows = slice(hd_i * 64, hd_i * 64 + 64)
                    o = OT.ap().rearrange("p (c t) -> p c t", c=NC)[rows, hp, qtb * 512:(qtb + 1) * 512]
                    with ALP(reason="bf16 attn out"):
                        return eng.tensor_tensor(o, psA.ap()[rows, :], ost.ap()[rows, 0:512], ALU.mult)
                P.op("vector", f_div, reads=[psA_b, ost_b], writes=[OT_b])

    def f_ldwo(e):
        return e.dma_start(out=wsl.ap()[:, 0:NC * 1024].rearrange("p (c d) -> p c d", c=NC),
                           in_=ctx.dram["w_o"].ap().transpose([1, 0, 2]))
    P.op("sync", f_ldwo, writes=[wsl_b], dma=True)

    for do in range(NC):
        for tg in range(4):
            psp, psp_b = ctx.ps[(do * 4 + tg) % 2]

            def f_omm(eng, do=do, tg=tg, psp=psp):
                last = None
                for hp in range(NC):
                    lhsT = wsl.ap()[:, hp * 1024 + do * 128: hp * 1024 + do * 128 + 128]
                    rhs = OT.ap().rearrange("p (c t) -> p c t", c=NC)[:, hp, tg * 512:(tg + 1) * 512]
                    last = eng.matmul(psp.ap(), lhsT, rhs, start=(hp == 0), stop=(hp == NC - 1))
                return last
            P.op("tensor", f_omm, reads=[OT_b, wsl_b], writes=[psp_b])

            def f_ldx(e, do=do, tg=tg):
                return e.dma_start(out=stg.ap()[:, 0:512],
                                   in_=x_dram.ap().rearrange("(c p) t -> p c t", p=128)[:, do, tg * 512:(tg + 1) * 512])
            P.op("sync", f_ldx, reads=[stg_b], writes=[stg_b], dma=True)

            def f_add(eng, do=do, psp=psp):
                return eng.scalar_tensor_tensor(ost.ap()[:, 512:1024], psp.ap(),
                                                bo.ap()[:, do:do + 1], stg.ap()[:, 0:512],
                                                ALU.add, ALU.add)
            P.op("vector", f_add, reads=[psp_b, bo_b, stg_b], writes=[ost_b])

            def f_st(e, do=do, tg=tg):
                return e.dma_start(out=ctx.y_out.ap().rearrange("(c p) t -> p c t", p=128)[:, do, tg * 512:(tg + 1) * 512],
                                   in_=ost.ap()[:, 512:1024])
            P.op("gpsimd", f_st, reads=[ost_b], dma=True)
